# version 02
# baseline (speedup 1.0000x reference)
import numpy as np
import concourse.bass as bass
import concourse.bass_utils as _bu
import concourse.mybir as mybir
from concourse.bass_utils import run_bass_kernel_spmd

if not getattr(_bu, "_ldwopt_patched", False):
    _orig_run_command = _bu.run_command
    def _run_command_ldwopt(argv, **kw):
        argv = ["--enable-ldw-opt=true" if a == "--enable-ldw-opt=false" else a for a in argv]
        return _orig_run_command(argv, **kw)
    _bu.run_command = _run_command_ldwopt
    _bu._ldwopt_patched = True
from concourse.bass_core_rust import IndirectOffsetOnAxis
from contextlib import ExitStack

F32 = mybir.dt.float32
BF16 = mybir.dt.bfloat16
I32 = mybir.dt.int32
AF = mybir.ActivationFunctionType
ALU = mybir.AluOpType
AXX = mybir.AxisListType.X

HID, NH, NKV, HD, S, B = 4096, 32, 8, 128, 2048, 4
SM = 1.0 / HD
EPS = 1e-5
NEG = -1e30

N_TILES = [2, 4, 6, 8, 9, 9, 10, 10]
N_RUN = [2, 4, 6, 8, 9, 9, 9, 9]
RUN_START = {0: [0, 0, 0, 0, 0, 2, 4, 6], 1: [0, 0, 0, 0, 1, 3, 5, 7]}
MASK_POS = []
for _i in range(8):
    _n = N_TILES[_i]
    if _i < 4:
        MASK_POS.append([(_n - 2) * 128, (_n - 1) * 128, 0])
    elif _i < 6:
        MASK_POS.append([0, 8 * 128, 128])
    else:
        MASK_POS.append([0, 128, 9 * 128])

DEBUG = False
ENGS = ("sync", "gpsimd", "tensor", "vector", "scalar")


class Prog:
    def __init__(self):
        self.items = {e: [] for e in ENGS}
        self.cnt = {}
        self.pending = {e: [] for e in ENGS}

    def emit(self, eng, fn, waits=(), inc=None, amt=1):
        w = [x for x in waits if x is not None] + self.pending[eng]
        self.pending[eng] = []
        self.items[eng].append((w, fn, inc, amt))
        if inc is not None:
            self.cnt[inc] = self.cnt.get(inc, 0) + amt
            return (inc, self.cnt[inc])
        return None

    def now(self, sem):
        return (sem, self.cnt.get(sem, 0))

    def barrier(self):
        toks = [(s, v) for s, v in self.cnt.items()]
        for e in ENGS:
            self.pending[e] = list(toks)

    def render(self, block, sems):
        def mk(name):
            def body(e):
                for waits, fn, inc, amt in self.items[name]:
                    seen = {}
                    for (s, v) in waits:
                        if v > 0 and seen.get(s, -1) < v:
                            seen[s] = v
                    for s, v in seen.items():
                        e.wait_ge(sems[s], v)
                    inst = fn(e) if fn is not None else None
                    if inc is not None:
                        inst.then_inc(sems[inc], amt)
            return body
        block.sync(mk("sync"))
        block.gpsimd(mk("gpsimd"))
        block.tensor(mk("tensor"))
        block.vector(mk("vector"))
        block.scalar(mk("scalar"))


def build_program(n_phases=3):
    nc = bass.Bass("TRN2", target_bir_lowering=False)

    x_hbm = nc.dram_tensor("x", [S, HID], F32, kind="ExternalInput").ap()
    xm_hbm = nc.dram_tensor("x_mine", [1024, HID], F32, kind="ExternalInput").ap()
    wkvT_h = nc.dram_tensor("wkvT", [HID, 2048], F32, kind="ExternalInput").ap()
    wqT_h = nc.dram_tensor("wqT", [HID, HID], F32, kind="ExternalInput").ap()
    wdT_h = nc.dram_tensor("wdT", [HID, HID], F32, kind="ExternalInput").ap()
    wpT_h = nc.dram_tensor("wpT", [HID, HID], F32, kind="ExternalInput").ap()
    wnT_h = nc.dram_tensor("wnT", [HID, HID], F32, kind="ExternalInput").ap()
    bkv_h = nc.dram_tensor("bkv", [128, 16], F32, kind="ExternalInput").ap()
    bq_h = nc.dram_tensor("bq", [128, 32], F32, kind="ExternalInput").ap()
    bd_h = nc.dram_tensor("bd", [128, 32], F32, kind="ExternalInput").ap()
    cosk_h = nc.dram_tensor("cosk", [128, S], F32, kind="ExternalInput").ap()
    sink_h = nc.dram_tensor("sink", [128, S], F32, kind="ExternalInput").ap()
    cosq_h = nc.dram_tensor("cosq", [128, 1024], F32, kind="ExternalInput").ap()
    sinq_h = nc.dram_tensor("sinq", [128, 1024], F32, kind="ExternalInput").ap()
    prot_h = nc.dram_tensor("prot", [128, 128], F32, kind="ExternalInput").ap()
    iden_h = nc.dram_tensor("iden", [128, 128], F32, kind="ExternalInput").ap()
    masks_h = nc.dram_tensor("masks", [8 * 3 * 128, 128], F32, kind="ExternalInput").ap()
    gtab_h = nc.dram_tensor("gtab", [128, 64], I32, kind="ExternalInput").ap()
    out_hbm = nc.dram_tensor("out", [1024, HID], F32, kind="ExternalOutput").ap()

    xn_dram = nc.dram_tensor("xn_dram", [HID, S], BF16, kind="Internal").ap()
    k_dram = nc.dram_tensor("k_dram", [NKV * 128, S], BF16, kind="Internal").ap()
    v_dram = nc.dram_tensor("v_dram", [NKV * 128, S], BF16, kind="Internal").ap()

    dbg = {}
    if DEBUG:
        dbg["xnT"] = nc.dram_tensor("dbg_xnT", [HID, S], BF16, kind="ExternalOutput").ap()
        dbg["k"] = nc.dram_tensor("dbg_k", [NKV * 128, S], BF16, kind="ExternalOutput").ap()
        dbg["v"] = nc.dram_tensor("dbg_v", [NKV * 128, S], BF16, kind="ExternalOutput").ap()
        dbg["ctx"] = nc.dram_tensor("dbg_ctx", [NH * 128, 1024], BF16, kind="ExternalOutput").ap()
        dbg["xnTh"] = nc.dram_tensor("dbg_xnTh", [HID, 1024], BF16, kind="ExternalOutput").ap()

    p = Prog()
    fin_tokens = []

    with ExitStack() as top:
        sems = {}
        for s_ in ("q", "g", "p", "v", "a", "sa", "sb", "xa", "xb", "ka", "kb", "va", "vb",
                   "wa", "wb", "pa", "pb", "na", "nb", "xr", "xw", "wk", "ck", "cg"):
            sems[s_] = top.enter_context(nc.semaphore("sem_" + s_))
        block = top.enter_context(nc.Block())

        sbt = lambda nm, shape, dt: top.enter_context(nc.sbuf_tensor(nm, shape, dt))
        bkv_sb = sbt("bkv_sb", [128, 16], F32)
        bq_sb = sbt("bq_sb", [128, 32], F32)
        bd_sb = sbt("bd_sb", [128, 32], F32)
        prot_sb = sbt("prot_sb", [128, 128], BF16)
        idb_sb = sbt("idb_sb", [128, 128], BF16)
        idf_sb = sbt("idf_sb", [128, 128], F32)
        gtab_sb = sbt("gtab_sb", [128, 64], I32)
        fence_sb = sbt("fence_sb", [128, 2], F32)
        p.emit("sync", lambda e: e.dma_start(out=bkv_sb[:], in_=bkv_h), inc="ck", amt=16)
        p.emit("sync", lambda e: e.dma_start(out=bq_sb[:], in_=bq_h), inc="ck", amt=16)
        p.emit("sync", lambda e: e.dma_start(out=bd_sb[:], in_=bd_h), inc="ck", amt=16)
        p.emit("sync", lambda e: e.dma_start(out=idf_sb[:], in_=iden_h), inc="ck", amt=16)
        p.emit("sync", lambda e: e.dma_start(out=gtab_sb[:], in_=gtab_h), inc="ck", amt=16)
        p.emit("gpsimd", lambda e: e.dma_start(out=prot_sb[:], in_=prot_h), inc="cg", amt=16)
        p.emit("gpsimd", lambda e: e.dma_start(out=idb_sb[:], in_=iden_h), inc="cg", amt=16)
        CK = ("ck", 5 * 16)
        CG = ("cg", 2 * 16)
        p0_toks = [CK, CG]

        def ln_tile(stv, x_buf, out_buf, waits_ld, prev_norm_tok, scr=None, war=()):
            scr = stv["scr"] if scr is None else scr
            t1 = p.emit("vector", lambda e: e.reduce_sum(stv["s1"][:], x_buf[:], axis=AXX),
                        waits=waits_ld, inc="v")
            t2 = p.emit("vector", lambda e, scr=scr: e.scalar_tensor_tensor(
                scr[:], x_buf[:], 0.0, x_buf[:], ALU.add, ALU.mult, accum_out=stv["s2"][:]),
                waits=list(war), inc="v")
            tm = p.emit("vector", lambda e: e.tensor_scalar(
                stv["m"][:], stv["s1"][:], 1.0 / HID, None, ALU.mult), waits=[t1], inc="v")
            tq = p.emit("vector", lambda e: e.tensor_tensor(
                stv["msq"][:], stv["m"][:], stv["m"][:], ALU.mult), waits=[tm], inc="v")
            tv0 = p.emit("vector", lambda e: e.scalar_tensor_tensor(
                stv["var"][:], stv["s2"][:], 1.0 / HID, stv["msq"][:], ALU.mult, ALU.subtract),
                waits=[t2, tq], inc="v")
            tv = p.emit("vector", lambda e: e.tensor_scalar(
                stv["var"][:], stv["var"][:], EPS, None, ALU.add), waits=[tv0], inc="v")
            tl = p.emit("scalar", lambda e: e.activation(stv["lnv"][:], stv["var"][:], AF.Ln),
                        waits=[tv], inc="a")
            ta = p.emit("scalar", lambda e: e.activation(
                stv["rstd"][:], stv["lnv"][:], AF.Exp, scale=-0.5), waits=[tl], inc="a")
            tb = p.emit("vector", lambda e: e.tensor_scalar(
                stv["nbias"][:], stv["m"][:], stv["rstd"][:], -1.0, ALU.mult, ALU.mult),
                waits=[ta], inc="v")
            return p.emit("scalar", lambda e: e.activation(
                out_buf[:], x_buf[:], AF.Identity, bias=stv["nbias"][:], scale=stv["rstd"][:]),
                waits=[tb] + list(war), inc="a")

        tok_kdram = tok_vdram = None

        with ExitStack() as ph:
            sb1 = lambda nm, shape, dt: ph.enter_context(nc.sbuf_tensor(nm, shape, dt))
            x_sb = [sb1(f"x_sb{i}", [128, HID], F32) for i in range(2)]
            xnn_sb = [sb1(f"xnn_sb{i}", [128, HID], BF16) for i in range(2)]
            xnT_sb = sb1("xnT_sb", [128, 32 * 256], BF16)
            wkv_sb = sb1("wkv_sb", [128, 32 * 1024], BF16)
            k_sb = sb1("k_sb", [128, NKV * S], BF16)
            vstage = sb1("vstage", [128, 512], BF16)
            cosk_sb = sb1("cosk_sb", [128, S], F32)
            sink_sb = sb1("sink_sb", [128, S], F32)
            rt1 = sb1("rt1", [128, S], BF16)
            rt2 = sb1("rt2", [128, S], BF16)
            stv = {}
            for nm in ("s1", "s2", "m", "msq", "var", "lnv", "rstd", "nbias"):
                stv[nm] = sb1("st_" + nm, [128, 1], F32)
            stv["scr"] = sb1("st_scr", [128, HID], F32)
            ps1 = ph.enter_context(nc.psum_tensor("ps1", [128, 2048], F32))
            pb1 = ph.enter_context(nc.psum_tensor("pb1", [128, 512], BF16))
            pb2 = ph.enter_context(nc.psum_tensor("pb2", [128, 512], BF16))
            pbx = [pb1, pb2]
            pb_free = [None, None]

            def ld_wkv(cols0, waits):
                def fn(e, cols0=cols0):
                    src = wkvT_h.rearrange("(hc p) f -> p hc f", p=128)
                    return e.dma_start(out=wkv_sb[:], in_=src[:, :, cols0:cols0 + 1024])
                return p.emit("gpsimd", fn, waits=waits, inc="wk", amt=16)

            tok_wkvK = ld_wkv(0, [])
            p.emit("sync", lambda e: e.dma_start(out=cosk_sb[:], in_=cosk_h), inc="q", amt=16)
            COSK = p.emit("sync", lambda e: e.dma_start(out=sink_sb[:], in_=sink_h), inc="q", amt=16)

            norm_tok = [None] * 16
            xload = [None] * 16
            tile_evac = [None] * 16
            kv_evac = {}
            mm_last_k = None
            xnn_tp_last = [None, None]

            for sti in range(8):
                for tt in range(2):
                    t = sti * 2 + tt
                    bi = t % 2
                    wl = [norm_tok[t - 2]] if t >= 2 else list(p0_toks)
                    xload[t] = p.emit("sync", lambda e, t=t, bi=bi: e.dma_start(
                        out=x_sb[bi][:], in_=x_hbm[t * 128:(t + 1) * 128, :]), waits=wl,
                        inc="xa" if bi == 0 else "xb", amt=16)
                    norm_tok[t] = ln_tile(stv, x_sb[bi], xnn_sb[bi], [xload[t]], None,
                                          war=[xnn_tp_last[bi]] if xnn_tp_last[bi] else [])
                    evs = []
                    for b8 in range(8):
                        pbb = pbx[b8 % 2]
                        wts = [norm_tok[t], pb_free[b8 % 2]] + ([CG] if t == 0 and b8 == 0 else [])
                        def tfn(e, b8=b8, pbb=pbb, bi=bi):
                            r = None
                            for j in range(4):
                                r = e.transpose(pbb[:, j * 128:(j + 1) * 128],
                                                xnn_sb[bi][:, (b8 * 4 + j) * 128:(b8 * 4 + j + 1) * 128],
                                                idb_sb[:])
                            return r
                        tp = p.emit("tensor", tfn, waits=wts, inc="p")
                        xnn_tp_last[bi] = tp
                        def efn(e, b8=b8, tt=tt, pbb=pbb):
                            dst = xnT_sb[:].rearrange("p (hc f) -> p hc f", hc=32)
                            return e.tensor_copy(dst[:, b8 * 4:(b8 + 1) * 4, tt * 128:(tt + 1) * 128], pbb[:])
                        ev = p.emit("vector", efn, waits=[tp], inc="v")
                        pb_free[b8 % 2] = ev
                        evs.append(ev)
                    tile_evac[t] = evs[-1]
                    def wfn(e, t=t, tt=tt):
                        dst = xn_dram.rearrange("(hc d) t -> d hc t", d=128)
                        src = xnT_sb[:].rearrange("p (hc f) -> p hc f", hc=32)
                        return e.dma_start(out=dst[:, :, t * 128:(t + 1) * 128],
                                           in_=src[:, :, tt * 128:(tt + 1) * 128])
                    p.emit("sync", wfn, waits=[evs[-1], evs[-2]], inc="xw", amt=16)
                for fc in range(8):
                    pk = ps1[:, (fc % 2) * 512:(fc % 2) * 512 + 256]
                    wts = [tile_evac[sti * 2 + 1], tok_wkvK, kv_evac.get(("K", sti, fc - 2)), CK]
                    def mfn(e, fc=fc, pk=pk):
                        r = None
                        for hc in range(32):
                            r = e.matmul(pk, wkv_sb[:, hc * 1024 + fc * 128: hc * 1024 + (fc + 1) * 128],
                                         xnT_sb[:, hc * 256:(hc + 1) * 256], start=(hc == 0), stop=(hc == 31))
                        return r
                    mm_last_k = p.emit("tensor", mfn, waits=wts, inc="p")
                    kv_evac[("K", sti, fc)] = p.emit("vector", lambda e, fc=fc, sti=sti, pk=pk: e.tensor_scalar(
                        k_sb[:, fc * S + sti * 256: fc * S + sti * 256 + 256], pk,
                        bkv_sb[:, fc:fc + 1], None, ALU.add), waits=[mm_last_k], inc="v")

            rope_last = None
            for g in range(NKV):
                for ch in range(4):
                    c0 = g * S + ch * 512
                    pr = ps1[:, 1024 + (ch % 2) * 512: 1024 + (ch % 2) * 512 + 512]
                    tr = p.emit("tensor", lambda e, c0=c0, pr=pr: e.matmul(
                        pr, prot_sb[:], k_sb[:, c0:c0 + 512], start=True, stop=True),
                        waits=[kv_evac[("K", 7, g)], p.now("v"), COSK], inc="p")
                    p.emit("vector", lambda e, c0=c0, ch=ch: e.tensor_tensor(
                        rt1[:, ch * 512:(ch + 1) * 512], k_sb[:, c0:c0 + 512],
                        cosk_sb[:, ch * 512:(ch + 1) * 512], ALU.mult))
                    p.emit("vector", lambda e, ch=ch, pr=pr: e.tensor_tensor(
                        rt2[:, ch * 512:(ch + 1) * 512], pr, sink_sb[:, ch * 512:(ch + 1) * 512], ALU.mult),
                        waits=[tr])
                    rope_last = p.emit("vector", lambda e, c0=c0, ch=ch: e.tensor_tensor(
                        k_sb[:, c0:c0 + 512], rt1[:, ch * 512:(ch + 1) * 512],
                        rt2[:, ch * 512:(ch + 1) * 512], ALU.add), inc="v")
            def kdfn(e):
                return e.dma_start(out=k_dram.rearrange("(g d) t -> d g t", d=128), in_=k_sb[:].rearrange("p (g t) -> p g t", g=8))
            tok_kdram = p.emit("sync", kdfn, waits=[rope_last], inc="xw", amt=16)
            if DEBUG:
                fin_tokens.append(p.emit("gpsimd", lambda e: e.dma_start(
                    out=dbg["k"].rearrange("(g d) t -> d g t", d=128), in_=k_sb[:].rearrange("p (g t) -> p g t", g=8)),
                    waits=[rope_last], inc="g", amt=16))
                fin_tokens.append(p.emit("gpsimd", lambda e: e.dma_start(out=dbg["xnT"], in_=xn_dram),
                                         waits=[tok_kdram], inc="g", amt=16))

            tok_wkvV = ld_wkv(1024, [mm_last_k])
            v_evac = {}
            xn_rd = [None] * 8
            for sti in range(8):
                wts = [tok_kdram, mm_last_k] if sti == 0 else [v_evac[(sti - 1, 7)]]
                def xrd(e, sti=sti):
                    src = xn_dram.rearrange("(hc d) t -> d hc t", d=128)
                    dst = xnT_sb[:].rearrange("p (hc f) -> p hc f", hc=32)
                    return e.dma_start(out=dst, in_=src[:, :, sti * 256:(sti + 1) * 256])
                xn_rd[sti] = p.emit("sync", xrd, waits=wts + [p.now("p")], inc="xr", amt=16)
                for fc in range(8):
                    pk = ps1[:, (fc % 2) * 512:(fc % 2) * 512 + 256]
                    wts = [xn_rd[sti], tok_wkvV, v_evac.get((sti, fc - 2))]
                    def mfn(e, fc=fc, pk=pk):
                        r = None
                        for hc in range(32):
                            r = e.matmul(pk, wkv_sb[:, hc * 1024 + fc * 128: hc * 1024 + (fc + 1) * 128],
                                         xnT_sb[:, hc * 256:(hc + 1) * 256], start=(hc == 0), stop=(hc == 31))
                        return r
                    mmv = p.emit("tensor", mfn, waits=wts, inc="p")
                    ev = p.emit("vector", lambda e, fc=fc, pk=pk: e.tensor_scalar(
                        vstage[:, 0:256], pk, bkv_sb[:, 8 + fc: 9 + fc], None, ALU.add), waits=[mmv], inc="v")
                    pbb = pbx[fc % 2]
                    def vtf(e, pbb=pbb):
                        r = None
                        for j in range(2):
                            r = e.transpose(pbb[:, j * 128:(j + 1) * 128], vstage[:, j * 128:(j + 1) * 128],
                                            idb_sb[:])
                        return r
                    tpv = p.emit("tensor", vtf, waits=[ev, pb_free[fc % 2]], inc="p")
                    v_evac[(sti, fc)] = p.emit("vector", lambda e, fc=fc, sti=sti, pbb=pbb: e.tensor_copy(
                        k_sb[:, fc * S + sti * 256: fc * S + sti * 256 + 256], pbb[:, 0:256]), waits=[tpv], inc="v")
                    pb_free[fc % 2] = v_evac[(sti, fc)]
            def vdfn(e):
                return e.dma_start(out=v_dram.rearrange("(g d) t -> d g t", d=128), in_=k_sb[:].rearrange("p (g t) -> p g t", g=8))
            tok_vdram = p.emit("sync", vdfn, waits=[v_evac[(7, 7)]], inc="xw", amt=16)
            if DEBUG:
                fin_tokens.append(p.emit("gpsimd", lambda e: e.dma_start(
                    out=dbg["v"].rearrange("(g d) t -> d g t", d=128), in_=k_sb[:].rearrange("p (g t) -> p g t", g=8)),
                    waits=[v_evac[(7, 7)]], inc="g", amt=16))
            p.barrier()

        if n_phases < 2:
            tok = p.emit("sync", lambda e: e.dma_start(out=out_hbm[0:128, :], in_=x_sb[0][:]),
                         inc="q", amt=16)
            p.emit("sync", None, waits=[tok] + fin_tokens)
            p.render(block, sems)
            return nc

        with ExitStack() as ph2:
            sb2 = lambda nm, shape, dt: ph2.enter_context(nc.sbuf_tensor(nm, shape, dt))
            xnT_half = sb2("xnT_half", [128, 32 * 1024], BF16)
            ctx_sb = sb2("ctx_sb", [128, 32 * 1024], BF16)
            ps2 = ph2.enter_context(nc.psum_tensor("ps2", [128, 3584], F32))
            pbT = ph2.enter_context(nc.psum_tensor("pbT", [128, 1024], BF16))
            SA0, SB0 = 1024, 2560

            with ExitStack() as pha:
                sba = lambda nm, shape, dt: pha.enter_context(nc.sbuf_tensor(nm, shape, dt))
                x2 = [sba(f"x2_{i}", [128, HID], F32) for i in range(2)]
                xnn2 = [sba(f"xnn2_{i}", [128, HID], F32) for i in range(2)]
                stv2 = {}
                for nm in ("s1", "s2", "m", "msq", "var", "lnv", "rstd", "nbias"):
                    stv2[nm] = sba("sa_" + nm, [128, 1], F32)
                norm2 = [None] * 8
                slot_free = [None, None]
                last_ev = None
                xnn2_tp_last = [None, None]
                for t in range(8):
                    bi = t % 2
                    wl = [norm2[t - 2]] if t >= 2 else []
                    tl = p.emit("sync", lambda e, t=t, bi=bi: e.dma_start(
                        out=x2[bi][:], in_=xm_hbm[t * 128:(t + 1) * 128, :]), waits=wl,
                        inc="xa" if bi == 0 else "xb", amt=16)
                    norm2[t] = ln_tile(stv2, x2[bi], xnn2[bi], [tl], None, scr=xnn2[bi],
                                       war=[xnn2_tp_last[bi]] if xnn2_tp_last[bi] else [])
                    for b8 in range(8):
                        sl = b8 % 2
                        po = ps2[:, sl * 512:(sl + 1) * 512]
                        def tfn(e, b8=b8, po=po, bi=bi):
                            r = None
                            for j in range(4):
                                r = e.transpose(po[:, j * 128:(j + 1) * 128],
                                                xnn2[bi][:, (b8 * 4 + j) * 128:(b8 * 4 + j + 1) * 128],
                                                idf_sb[:])
                            return r
                        tp = p.emit("tensor", tfn, waits=[norm2[t], slot_free[sl]]
                                    + ([CK] if t == 0 and b8 == 0 else []), inc="p")
                        xnn2_tp_last[bi] = tp
                        def efn(e, b8=b8, t=t, po=po):
                            dst = xnT_half[:].rearrange("p (hc f) -> p hc f", hc=32)
                            return e.tensor_copy(dst[:, b8 * 4:(b8 + 1) * 4, t * 128:(t + 1) * 128], po[:])
                        last_ev = p.emit("vector", efn, waits=[tp], inc="v")
                        slot_free[sl] = last_ev
                xnT_ready = [last_ev]
                p.barrier()
            if DEBUG:
                fin_tokens.append(p.emit("gpsimd", lambda e: e.dma_start(
                    out=dbg["xnTh"].rearrange("(hc d) t -> d hc t", d=128), in_=xnT_half[:].rearrange("p (hc t) -> p hc t", hc=32)),
                    inc="g", amt=16))

            with ExitStack() as phb:
                sbb = lambda nm, shape, dt: phb.enter_context(nc.sbuf_tensor(nm, shape, dt))
                q_sb = sbb("q_sb", [128, 4 * 1024], BF16)
                wq_ch = [sbb(f"wq_ch{i}", [128, 32 * 128], BF16) for i in range(2)]
                masks_sb = sbb("masks_sb", [128, 24 * 128], BF16)
                cosq_sb = sbb("cosq_sb", [128, 1024], F32)
                sinq_sb = sbb("sinq_sb", [128, 1024], F32)
                p.emit("sync", lambda e: e.dma_start(out=cosq_sb[:], in_=cosq_h), inc="ck", amt=16)
                p.emit("sync", lambda e: e.dma_start(out=sinq_sb[:], in_=sinq_h), inc="ck", amt=16)
                CKQ = ("ck", 7 * 16)
                p.emit("gpsimd", lambda e: e.dma_start(
                    out=masks_sb[:].rearrange("p (r k) -> p r k", r=24),
                    in_=masks_h.rearrange("(r p) k -> p r k", p=128)), inc="cg", amt=16)
                CGM = ("cg", 3 * 16)
                kg_sb = [sbb(f"kg{i}", [128, 1280], BF16) for i in range(2)]
                vg_sb = [sbb(f"vg{i}", [128, 1280], BF16) for i in range(2)]
                probs_sb = [sbb(f"pr{i}", [128, 1280], BF16) for i in range(2)]
                probsN_sb = [sbb(f"pn{i}", [128, 1280], BF16) for i in range(2)]
                probsT_sb = [sbb(f"pt{i}", [128, 1280], BF16) for i in range(2)]
                rq1 = sbb("rq1", [128, 1024], BF16)
                rq2 = sbb("rq2", [128, 1024], BF16)
                lt = [sbb(f"lt{i}", [128, 1], F32) for i in range(2)]
                nmax = [sbb(f"nmax{i}", [128, 1], F32) for i in range(2)]
                ebias = [sbb(f"eb{i}", [128, 1], F32) for i in range(2)]
                rinv = [sbb(f"ri{i}", [128, 1], F32) for i in range(2)]

                mload = CGM

                ATT_ORDER = []
                for j in range(4):
                    for pr_ in ((0, 4), (1, 5), (2, 6), (3, 7)):
                        ATT_ORDER.append((pr_[0], j, "B"))
                        ATT_ORDER.append((pr_[1], j, "A"))

                kv_ready = [tok_kdram, tok_vdram]
                gath_tok = {}
                kg_free = [None, None]
                vg_free = [None, None]
                probs_free = [None, None]
                probsN_free = [None, None]
                probsT_free = [None, None]
                ctx_slot_free = [None, None]
                score_free = {"A": None, "B": None}
                pbT_free = None
                wq_tok = {}
                last_cev = None

                def emit_gathers(g, idx):
                    buf = idx % 2
                    i = ATT_ORDER[idx][0]
                    n, nr = N_TILES[i], N_RUN[i]
                    off = (n - nr) * 128
                    ksem = "ka" if buf == 0 else "kb"
                    vsem = "va" if buf == 0 else "vb"
                    tk = p.emit("gpsimd", lambda e, g=g, i=i, off=off, nr=nr, buf=buf: e.indirect_dma_start(
                        out=kg_sb[buf][:, off:off + nr * 128], out_offset=None,
                        in_=k_dram,
                        in_offset=IndirectOffsetOnAxis(gtab_sb[:, g * 8 + i:g * 8 + i + 1], 1)),
                        waits=kv_ready + [kg_free[buf], CK], inc=ksem, amt=16)
                    tv_ = p.emit("gpsimd", lambda e, g=g, i=i, off=off, nr=nr, buf=buf: e.indirect_dma_start(
                        out=vg_sb[buf][:, off:off + nr * 128], out_offset=None,
                        in_=v_dram,
                        in_offset=IndirectOffsetOnAxis(gtab_sb[:, g * 8 + i:g * 8 + i + 1], 1)),
                        waits=[vg_free[buf]], inc=vsem, amt=16)
                    if i >= 6:
                        tk = p.emit("gpsimd", lambda e, g=g, buf=buf: e.dma_start(
                            out=kg_sb[buf][:, 0:128], in_=k_dram[g * 128:(g + 1) * 128, 384:512]),
                            inc=ksem, amt=16)
                        tv_ = p.emit("gpsimd", lambda e, g=g, buf=buf: e.dma_start(
                            out=vg_sb[buf][:, 0:128], in_=v_dram[g * 128:(g + 1) * 128, 384:512]),
                            inc=vsem, amt=16)
                    gath_tok[(g, idx)] = (tk, tv_)

                emit_gathers(0, 0)
                emit_gathers(0, 1)

                for g in range(NKV):
                    q_rope = [None] * 4
                    for fc in range(4):
                        wb = fc % 2
                        prev_use = wq_tok.get((g, fc - 2)) or wq_tok.get((g - 1, fc + 2))
                        def wqfn(e, g=g, fc=fc, wb=wb):
                            src = wqT_h.rearrange("(hc p) f -> p hc f", p=128)
                            return e.dma_start(out=wq_ch[wb][:],
                                               in_=src[:, :, g * 512 + fc * 128: g * 512 + (fc + 1) * 128])
                        lw = p.emit("gpsimd", wqfn, waits=[prev_use], inc="wa" if wb == 0 else "wb", amt=16)
                        def mfn(e, wb=wb):
                            r = None
                            for hc in range(32):
                                for ts2 in range(2):
                                    r = e.matmul(ps2[:, ts2 * 512:(ts2 + 1) * 512],
                                                 wq_ch[wb][:, hc * 128:(hc + 1) * 128],
                                                 xnT_half[:, hc * 1024 + ts2 * 512: hc * 1024 + ts2 * 512 + 512],
                                                 start=(hc == 0), stop=(hc == 31))
                            return r
                        mm = p.emit("tensor", mfn, waits=[lw, p.now("v")] + xnT_ready, inc="p")
                        for ts2 in range(2):
                            p.emit("vector", lambda e, fc=fc, ts2=ts2, g=g: e.tensor_scalar(
                                q_sb[:, fc * 1024 + ts2 * 512: fc * 1024 + ts2 * 512 + 512],
                                ps2[:, ts2 * 512:(ts2 + 1) * 512],
                                bq_sb[:, g * 4 + fc: g * 4 + fc + 1], None, ALU.add), waits=[mm], inc="v")
                        wq_tok[(g, fc)] = mm
                        ev_tok = p.now("v")
                        rots = []
                        for ch in range(2):
                            pk = ps2[:, ch * 512:(ch + 1) * 512]
                            rots.append(p.emit("tensor", lambda e, fc=fc, ch=ch, pk=pk: e.matmul(
                                pk, prot_sb[:], q_sb[:, fc * 1024 + ch * 512: fc * 1024 + ch * 512 + 512],
                                start=True, stop=True), waits=[ev_tok], inc="p"))
                        p.emit("vector", lambda e, fc=fc: e.tensor_tensor(
                            rq1[:], q_sb[:, fc * 1024:(fc + 1) * 1024], cosq_sb[:], ALU.mult),
                            waits=[ev_tok, CKQ])
                        for ch in range(2):
                            p.emit("vector", lambda e, ch=ch, pk=ps2[:, ch * 512:(ch + 1) * 512]: e.tensor_tensor(
                                rq2[:, ch * 512:(ch + 1) * 512], pk,
                                sinq_sb[:, ch * 512:(ch + 1) * 512], ALU.mult), waits=[rots[ch]])
                        q_rope[fc] = p.emit("vector", lambda e, fc=fc: e.tensor_tensor(
                            q_sb[:, fc * 1024:(fc + 1) * 1024], rq1[:], rq2[:], ALU.add), inc="v")

                    for idx, (i, j, set_) in enumerate(ATT_ORDER):
                        buf = idx % 2
                        n = N_TILES[i]
                        ncols = n * 128
                        s0 = SA0 if set_ == "A" else SB0
                        wts = [gath_tok[(g, idx)][0], q_rope[j], score_free[set_], mload]
                        banks_seen = set()
                        def maskmm(e, i=i, s0=s0):
                            r = None
                            seen = set()
                            for sl in range(3):
                                pos = MASK_POS[i][sl]
                                bk = pos // 512
                                st = bk not in seen
                                seen.add(bk)
                                r = e.matmul(ps2[:, s0 + pos: s0 + pos + 128], idb_sb[:],
                                             masks_sb[:, (i * 3 + sl) * 128:(i * 3 + sl + 1) * 128],
                                             start=st, stop=True)
                            return r
                        p.emit("tensor", maskmm, waits=wts)
                        mask_banks = {MASK_POS[i][sl] // 512 for sl in range(3)}
                        def scoremm(e, i=i, j=j, s0=s0, buf=buf, ncols=ncols, mask_banks=mask_banks):
                            r = None
                            c = 0
                            while c < ncols:
                                w = min(512, ncols - c)
                                st = (c // 512) not in mask_banks
                                r = e.matmul(ps2[:, s0 + c: s0 + c + w],
                                             q_sb[:, j * 1024 + i * 128: j * 1024 + (i + 1) * 128],
                                             kg_sb[buf][:, c:c + w], start=st, stop=True)
                                c += w
                            return r
                        sc = p.emit("tensor", scoremm, inc="p")
                        kg_free[buf] = sc
                        nm_ = p.emit("vector", lambda e, s0=s0, ncols=ncols, buf=buf: e.reduce_max(
                            nmax[buf][:], ps2[:, s0:s0 + ncols], axis=AXX, negate=True), waits=[sc], inc="v")
                        ex = p.emit("scalar", lambda e, s0=s0, ncols=ncols, buf=buf: e.activation(
                            probs_sb[buf][:, 0:ncols], ps2[:, s0:s0 + ncols], AF.Exp,
                            bias=nmax[buf][:], accum_out=lt[buf][:]),
                            waits=[nm_, probs_free[buf]], inc="a")
                        score_free[set_] = ex
                        p.emit("scalar", lambda e, buf=buf: e.activation(
                            ebias[buf][:], lt[buf][:], AF.Ln))
                        tri = p.emit("scalar", lambda e, buf=buf: e.activation(
                            rinv[buf][:], ebias[buf][:], AF.Exp, scale=-1.0), inc="a")
                        pn = p.emit("scalar", lambda e, buf=buf, ncols=ncols: e.activation(
                            probsN_sb[buf][:, 0:ncols], probs_sb[buf][:, 0:ncols], AF.Identity,
                            scale=rinv[buf][:]),
                            waits=[probsN_free[buf], tri], inc="a")
                        probs_free[buf] = pn
                        tev = []
                        tps = []
                        nb = (n + 7) // 8
                        for b8 in range(nb):
                            t0, t1 = b8 * 8, min(n, b8 * 8 + 8)
                            wts_t = [pn] + ([tev[-1]] if b8 else [pbT_free])
                            def tfn(e, t0=t0, t1=t1, buf=buf):
                                r = None
                                for tt in range(t0, t1):
                                    r = e.transpose(pbT[:, (tt - t0) * 128:(tt - t0 + 1) * 128],
                                                    probsN_sb[buf][:, tt * 128:(tt + 1) * 128], idb_sb[:])
                                return r
                            tp = p.emit("tensor", tfn, waits=wts_t, inc="p")
                            tps.append(tp)
                            tev.append(p.emit("vector", lambda e, t0=t0, t1=t1, buf=buf: e.tensor_copy(
                                probsT_sb[buf][:, t0 * 128:t1 * 128], pbT[:, 0:(t1 - t0) * 128]),
                                waits=[tp, probsT_free[buf] if b8 == 0 else None], inc="v"))
                        probsN_free[buf] = tps[-1]
                        pbT_free = tev[-1]
                        cslot = idx % 2
                        cps = ps2[:, cslot * 512: cslot * 512 + 128]
                        def cfn(e, n=n, buf=buf, cps=cps):
                            r = None
                            for tt in range(n):
                                r = e.matmul(cps, vg_sb[buf][:, tt * 128:(tt + 1) * 128],
                                             probsT_sb[buf][:, tt * 128:(tt + 1) * 128],
                                             start=(tt == 0), stop=(tt == n - 1))
                            return r
                        cx = p.emit("tensor", cfn, waits=[tev[-1], gath_tok[(g, idx)][1],
                                                          ctx_slot_free[cslot]], inc="p")
                        probsT_free[buf] = cx
                        vg_free[buf] = cx
                        h = g * 4 + j
                        cev = p.emit("vector", lambda e, h=h, i=i, cps=cps: e.tensor_copy(
                            ctx_sb[:, h * 1024 + i * 128: h * 1024 + (i + 1) * 128], cps),
                            waits=[cx], inc="v")
                        ctx_slot_free[cslot] = cev
                        last_cev = cev
                        nxt = g * 32 + idx + 2
                        if nxt < NKV * 32:
                            emit_gathers(nxt // 32, nxt % 32)
                p.barrier()
            if DEBUG:
                fin_tokens.append(p.emit("gpsimd", lambda e: e.dma_start(
                    out=dbg["ctx"].rearrange("(h d) t -> d h t", d=128), in_=ctx_sb[:].rearrange("p (h t) -> p h t", h=32)),
                    inc="g", amt=16))

            if n_phases < 3:
                tok = p.emit("gpsimd", lambda e: e.dma_start(out=out_hbm[0:128, :], in_=xnT_half[:, 0:4096]),
                             inc="g", amt=16)
                p.emit("sync", None, waits=[tok] + fin_tokens)
                p.render(block, sems)
                return nc

            with ExitStack() as ph3:
                sb3 = lambda nm, shape, dt: ph3.enter_context(nc.sbuf_tensor(nm, shape, dt))
                wd_ch = [sb3(f"wd{i}", [128, 32 * 128], BF16) for i in range(2)]
                cwp = [sb3(f"cwp{i}", [128, 32 * 128], BF16) for i in range(2)]
                cwn = [sb3(f"cwn{i}", [128, 32 * 128], BF16) for i in range(2)]
                wcomb = [sb3(f"wc{i}", [128, 32 * 128], BF16) for i in range(2)]
                stepn = sb3("stepn", [128, 16 * 128], BF16)
                g_sb = sb3("g_sb", [128, 512], F32)
                sg_sb = sb3("sg_sb", [128, 1024], F32)
                xch = [sb3(f"xch{i}", [128, 128], F32) for i in range(2)]
                och = [sb3(f"och{i}", [128, 128], F32) for i in range(2)]

                ldd, ldpn, wc_tok, dn_tok, ff_tok = {}, {}, {}, {}, {}
                sg_tok, sil_tok = {}, {}
                xadd = {}
                xbuf_free = {}
                och_free = {}
                po_free = {}
                out_cnt = 0

                def cmb_loads(fc):
                    wb = fc % 2
                    ldd[fc] = p.emit("gpsimd", lambda e, fc=fc, wb=wb: e.dma_start(
                        out=wd_ch[wb][:],
                        in_=wdT_h.rearrange("(hc p) f -> p hc f", p=128)[:, :, fc * 128:(fc + 1) * 128]),
                        waits=[dn_tok.get(fc - 2)], inc="wa" if wb == 0 else "wb", amt=16)
                    ldp = p.emit("gpsimd", lambda e, fc=fc, wb=wb: e.dma_start(
                        out=cwp[wb][:],
                        in_=wpT_h.rearrange("(hc p) f -> p hc f", p=128)[:, :, fc * 128:(fc + 1) * 128]),
                        waits=[wc_tok.get(fc - 2)], inc="pa" if wb == 0 else "pb", amt=16)
                    ldn = p.emit("gpsimd", lambda e, fc=fc, wb=wb: e.dma_start(
                        out=cwn[wb][:],
                        in_=wnT_h.rearrange("(hc p) f -> p hc f", p=128)[:, :, fc * 128:(fc + 1) * 128]),
                        inc="na" if wb == 0 else "nb", amt=16)
                    ldpn[fc] = (ldp, ldn)

                cmb_loads(0)
                cmb_loads(1)
                for fc in range(32):
                    wb = fc % 2
                    for hh in range(2):
                        tsn = p.emit("vector", lambda e, wb=wb, hh=hh: e.tensor_scalar(
                            stepn[:], cwn[wb][:, hh * 2048:(hh + 1) * 2048], 0.0, None, ALU.is_gt),
                            waits=[ldpn[fc][0], ldpn[fc][1], ff_tok.get(fc - 2)] if hh == 0 else [], inc="v")
                        wc_tok[fc] = p.emit("vector", lambda e, wb=wb, hh=hh: e.scalar_tensor_tensor(
                            wcomb[wb][:, hh * 2048:(hh + 1) * 2048], cwp[wb][:, hh * 2048:(hh + 1) * 2048],
                            0.0, stepn[:], ALU.is_gt, ALU.subtract), waits=[tsn], inc="v")
                    for ts2 in range(2):
                        pk = ps2[:, ts2 * 512:(ts2 + 1) * 512]
                        def dfn(e, wb=wb, ts2=ts2, pk=pk):
                            r = None
                            for hc in range(32):
                                r = e.matmul(pk, wd_ch[wb][:, hc * 128:(hc + 1) * 128],
                                             ctx_sb[:, hc * 1024 + ts2 * 512: hc * 1024 + ts2 * 512 + 512],
                                             start=(hc == 0), stop=(hc == 31))
                            return r
                        dn_tok[fc] = p.emit("tensor", dfn,
                                            waits=[ldd[fc], sg_tok.get((fc - 1, ts2))], inc="p")
                        pf = ps2[:, 1024 + ts2 * 512: 1024 + ts2 * 512 + 512]
                        def ffn_(e, wb=wb, ts2=ts2, pf=pf):
                            r = None
                            for hc in range(32):
                                r = e.matmul(pf, wcomb[wb][:, hc * 128:(hc + 1) * 128],
                                             xnT_half[:, hc * 1024 + ts2 * 512: hc * 1024 + ts2 * 512 + 512],
                                             start=(hc == 0), stop=(hc == 31))
                            return r
                        ff_tok[fc] = p.emit("tensor", ffn_,
                                            waits=[wc_tok[fc], sil_tok.get((fc - 1, ts2))], inc="p")
                        sil_tok[(fc, ts2)] = p.emit("scalar", lambda e, pf=pf: e.activation(
                            g_sb[:], pf, AF.Silu),
                            waits=[ff_tok[fc]] + ([sg_tok[(fc, 0)]] if ts2
                                                  else ([sg_tok[(fc - 1, 1)]] if fc else [])), inc="a")
                        p.emit("vector", lambda e, fc=fc, ts2=ts2, pk=pk: e.scalar_tensor_tensor(
                            sg_sb[:, ts2 * 512:(ts2 + 1) * 512], pk, bd_sb[:, fc:fc + 1],
                            g_sb[:], ALU.add, ALU.mult),
                            waits=[dn_tok[fc], sil_tok[(fc, ts2)]], inc="v")
                        sg_tok[(fc, ts2)] = p.emit("vector", lambda e: e.tensor_copy(
                            fence_sb[:, 0:1], fence_sb[:, 1:2]), inc="v")
                    for tt in range(8):
                        slot = tt % 2
                        po = ps2[:, 2048 + slot * 512: 2048 + slot * 512 + 128]
                        xb = tt % 2
                        seq = fc * 8 + tt
                        xld = p.emit("sync", lambda e, fc=fc, tt=tt, xb=xb: e.dma_start(
                            out=xch[xb][:], in_=xm_hbm[tt * 128:(tt + 1) * 128, fc * 128:(fc + 1) * 128]),
                            waits=[xbuf_free.get(xb)], inc="xa" if xb == 0 else "xb", amt=16)
                        p.emit("tensor", lambda e, tt=tt, po=po: e.transpose(
                            po, sg_sb[:, tt * 128:(tt + 1) * 128], idf_sb[:]),
                            waits=[sg_tok[(fc, tt // 4)], po_free.get(slot)], inc="p")
                        tp = p.emit("tensor", lambda e, tt=tt, po2=ps2[:, 2048 + slot * 512 + 128: 2048 + slot * 512 + 256]: e.transpose(
                            po2, sg_sb[:, tt * 128:(tt + 1) * 128], idf_sb[:]), inc="p")
                        ad = p.emit("vector", lambda e, slot=slot, po=po, xb=xb: e.tensor_tensor(
                            och[slot][:], po, xch[xb][:], ALU.add),
                            waits=[tp, xld, och_free.get(slot)], inc="v")
                        po_free[slot] = ad
                        xbuf_free[xb] = ad
                        st_ = p.emit("scalar", lambda e, fc=fc, tt=tt, slot=slot: e.dma_start(
                            out=out_hbm[tt * 128:(tt + 1) * 128, fc * 128:(fc + 1) * 128], in_=och[slot][:]),
                            waits=[ad], inc="sa" if slot == 0 else "sb", amt=16)
                        och_free[slot] = st_
                        out_cnt += 16
                    if fc + 2 < 32:
                        cmb_loads(fc + 2)

                p.emit("scalar", None, waits=[("sa", out_cnt // 2), ("sb", out_cnt // 2)] + fin_tokens)
                p.emit("sync", None, waits=[("sa", out_cnt // 2), ("sb", out_cnt // 2)])
                p.render(block, sems)

    return nc



def _tables():
    inv = 1.0 / (1e6 ** (np.arange(0, HD, 2, dtype=np.float64) / HD))
    t = np.arange(S, dtype=np.float64)
    fr = np.outer(inv, t)
    cosk = np.concatenate([np.cos(fr), np.cos(fr)], 0).astype(np.float32)
    sink = np.concatenate([np.sin(fr), np.sin(fr)], 0).astype(np.float32)
    return cosk, sink


def _masks(parity):
    causal = np.where(np.tril(np.ones((128, 128), bool)), 0.0, NEG).astype(np.float32)
    full = np.full((128, 128), NEG, np.float32)
    zero = np.zeros((128, 128), np.float32)
    m = np.zeros((8, 3, 128, 128), np.float32)
    for i in range(8):
        pg = 2 * i + parity
        if i < 4:
            if parity == 0:
                m[i, 0], m[i, 1], m[i, 2] = causal, full, zero
            else:
                m[i, 0], m[i, 1], m[i, 2] = zero, causal, zero
        else:
            ext = np.zeros((128, 128), np.float32)
            ext[:, 0:64] = NEG
            if (2 * pg - 14) % 8 != 0:
                ext[64:128, 64:128] = NEG
            if i < 6:
                m[i, 0], m[i, 1], m[i, 2] = ext, causal, zero
            else:
                vert = np.zeros((128, 128), np.float32)
                vert[:, 0:64] = NEG
                m[i, 0], m[i, 1], m[i, 2] = vert, ext, causal
    return np.ascontiguousarray(m.reshape(8 * 3 * 128, 128))


def make_inputs(L0_attn_hidden_states, L0_attn_w_qkv, L0_attn_b_qkv, L0_attn_w_dense,
                L0_attn_b_dense, L0_ffn_w_pos, L0_ffn_w_neg):
    x_full = np.ascontiguousarray(np.asarray(L0_attn_hidden_states, dtype=np.float32))
    w_qkv = np.asarray(L0_attn_w_qkv, dtype=np.float32)
    b_qkv = np.asarray(L0_attn_b_qkv, dtype=np.float32)
    qrows = np.concatenate([np.arange(g * 768, g * 768 + 512) for g in range(NKV)])
    krows = np.concatenate([np.arange(g * 768 + 512, g * 768 + 640) for g in range(NKV)])
    vrows = np.concatenate([np.arange(g * 768 + 640, g * 768 + 768) for g in range(NKV)])
    wqT = np.ascontiguousarray(w_qkv[qrows].T)
    wkvT = np.ascontiguousarray(np.concatenate([w_qkv[krows], w_qkv[vrows]], 0).T)
    wdT = np.ascontiguousarray(np.asarray(L0_attn_w_dense, np.float32).T)
    wpT = np.ascontiguousarray(np.asarray(L0_ffn_w_pos, np.float32).T)
    wnT = np.ascontiguousarray(np.asarray(L0_ffn_w_neg, np.float32).T)
    bkv_re = np.ascontiguousarray(np.concatenate([b_qkv[krows], b_qkv[vrows]]).reshape(16, 128).T)
    bq_re = np.ascontiguousarray(b_qkv[qrows].reshape(32, 128).T)
    bd_re = np.ascontiguousarray(np.asarray(L0_attn_b_dense, np.float32).reshape(32, 128).T)
    cosk, sink = _tables()
    prot = np.zeros((128, 128), np.float32)
    for mm in range(64):
        prot[mm + 64, mm] = -1.0
        prot[mm, mm + 64] = 1.0
    iden = np.eye(128, dtype=np.float32)

    in_maps = []
    for c in range(8):
        b, parity = c // 2, c % 2
        rows = np.concatenate([np.arange(128 * (2 * i + parity), 128 * (2 * i + parity) + 128)
                               for i in range(8)])
        gtab = np.empty((128, 64), np.int32)
        for g in range(8):
            for i in range(8):
                gtab[:, g * 8 + i] = (g * 128 + np.arange(128, dtype=np.int32)) * S                     + RUN_START[parity][i] * 128
        in_maps.append(dict(
            x=x_full[b], x_mine=np.ascontiguousarray(x_full[b][rows]),
            wkvT=wkvT, wqT=wqT, wdT=wdT, wpT=wpT, wnT=wnT,
            bkv=bkv_re, bq=bq_re, bd=bd_re, cosk=cosk, sink=sink,
            cosq=np.ascontiguousarray(cosk[:, rows] * np.float32(SM)),
            sinq=np.ascontiguousarray(sink[:, rows] * np.float32(SM)),
            prot=prot, iden=iden, masks=_masks(parity), gtab=gtab))
    return in_maps


def kernel(L0_attn_hidden_states, L0_attn_w_qkv, L0_attn_b_qkv, L0_attn_w_dense,
           L0_attn_b_dense, L0_ffn_w_pos, L0_ffn_w_neg):
    in_maps = make_inputs(L0_attn_hidden_states, L0_attn_w_qkv, L0_attn_b_qkv,
                          L0_attn_w_dense, L0_attn_b_dense, L0_ffn_w_pos, L0_ffn_w_neg)
    nc = build_program()
    res = run_bass_kernel_spmd(nc, in_maps, core_ids=list(range(8)))
    out = np.empty((B, S, HID), np.float32)
    for c in range(8):
        b, parity = c // 2, c % 2
        r = res.results[c]["out"]
        for i in range(8):
            out[b, 128 * (2 * i + parity):128 * (2 * i + parity) + 128, :] = \
                r[i * 128:(i + 1) * 128, :]
    return out


# version 03
# speedup vs baseline: 1.0676x; 1.0676x over previous
import numpy as np
import concourse.bass as bass
import concourse.bass_utils as _bu
import concourse.mybir as mybir
from concourse.bass_utils import run_bass_kernel_spmd

if not getattr(_bu, "_ldwopt_patched", False):
    _orig_run_command = _bu.run_command
    def _run_command_ldwopt(argv, **kw):
        argv = ["--enable-ldw-opt=true" if a == "--enable-ldw-opt=false" else a for a in argv]
        return _orig_run_command(argv, **kw)
    _bu.run_command = _run_command_ldwopt
    _bu._ldwopt_patched = True
from concourse.bass_core_rust import IndirectOffsetOnAxis
from contextlib import ExitStack

F32 = mybir.dt.float32
BF16 = mybir.dt.bfloat16
I32 = mybir.dt.int32
AF = mybir.ActivationFunctionType
ALU = mybir.AluOpType
AXX = mybir.AxisListType.X

HID, NH, NKV, HD, S, B = 4096, 32, 8, 128, 2048, 4
SM = 1.0 / HD
EPS = 1e-5
NEG = -1e30

N_TILES = [2, 4, 6, 8, 9, 9, 10, 10]
N_RUN = [2, 4, 6, 8, 9, 9, 9, 9]
RUN_START = {0: [0, 0, 0, 0, 0, 2, 4, 6], 1: [0, 0, 0, 0, 1, 3, 5, 7]}
MASK_POS = []
for _i in range(8):
    _n = N_TILES[_i]
    if _i < 4:
        MASK_POS.append([(_n - 2) * 128, (_n - 1) * 128, 0])
    elif _i < 6:
        MASK_POS.append([0, 8 * 128, 128])
    else:
        MASK_POS.append([0, 128, 9 * 128])

DEBUG = False
ENGS = ("sync", "gpsimd", "tensor", "vector", "scalar")


class Prog:
    def __init__(self):
        self.items = {e: [] for e in ENGS}
        self.cnt = {}
        self.pending = {e: [] for e in ENGS}

    def emit(self, eng, fn, waits=(), inc=None, amt=1):
        w = [x for x in waits if x is not None] + self.pending[eng]
        self.pending[eng] = []
        self.items[eng].append((w, fn, inc, amt))
        if inc is not None:
            self.cnt[inc] = self.cnt.get(inc, 0) + amt
            return (inc, self.cnt[inc])
        return None

    def now(self, sem):
        return (sem, self.cnt.get(sem, 0))

    def barrier(self):
        toks = [(s, v) for s, v in self.cnt.items()]
        for e in ENGS:
            self.pending[e] = list(toks)

    def render(self, block, sems):
        def mk(name):
            def body(e):
                for waits, fn, inc, amt in self.items[name]:
                    seen = {}
                    for (s, v) in waits:
                        if v > 0 and seen.get(s, -1) < v:
                            seen[s] = v
                    for s, v in seen.items():
                        e.wait_ge(sems[s], v)
                    inst = fn(e) if fn is not None else None
                    if inc is not None:
                        inst.then_inc(sems[inc], amt)
            return body
        block.sync(mk("sync"))
        block.gpsimd(mk("gpsimd"))
        block.tensor(mk("tensor"))
        block.vector(mk("vector"))
        block.scalar(mk("scalar"))


def build_program(n_phases=3):
    nc = bass.Bass("TRN2", target_bir_lowering=False)

    x_hbm = nc.dram_tensor("x", [S, HID], F32, kind="ExternalInput").ap()
    xm_hbm = nc.dram_tensor("x_mine", [1024, HID], F32, kind="ExternalInput").ap()
    wkvT_h = nc.dram_tensor("wkvT", [HID, 2048], F32, kind="ExternalInput").ap()
    wqT_h = nc.dram_tensor("wqT", [HID, HID], F32, kind="ExternalInput").ap()
    wdT_h = nc.dram_tensor("wdT", [HID, HID], F32, kind="ExternalInput").ap()
    wpT_h = nc.dram_tensor("wpT", [HID, HID], F32, kind="ExternalInput").ap()
    wnT_h = nc.dram_tensor("wnT", [HID, HID], F32, kind="ExternalInput").ap()
    bkv_h = nc.dram_tensor("bkv", [128, 16], F32, kind="ExternalInput").ap()
    bq_h = nc.dram_tensor("bq", [128, 32], F32, kind="ExternalInput").ap()
    bd_h = nc.dram_tensor("bd", [128, 32], F32, kind="ExternalInput").ap()
    cosk_h = nc.dram_tensor("cosk", [128, S], F32, kind="ExternalInput").ap()
    sink_h = nc.dram_tensor("sink", [128, S], F32, kind="ExternalInput").ap()
    cosq_h = nc.dram_tensor("cosq", [128, 1024], F32, kind="ExternalInput").ap()
    sinq_h = nc.dram_tensor("sinq", [128, 1024], F32, kind="ExternalInput").ap()
    prot_h = nc.dram_tensor("prot", [128, 128], F32, kind="ExternalInput").ap()
    iden_h = nc.dram_tensor("iden", [128, 128], F32, kind="ExternalInput").ap()
    masks_h = nc.dram_tensor("masks", [8 * 3 * 128, 128], F32, kind="ExternalInput").ap()
    gtab_h = nc.dram_tensor("gtab", [128, 64], I32, kind="ExternalInput").ap()
    out_hbm = nc.dram_tensor("out", [1024, HID], F32, kind="ExternalOutput").ap()

    xn_dram = nc.dram_tensor("xn_dram", [HID, S], BF16, kind="Internal").ap()
    k_dram = nc.dram_tensor("k_dram", [NKV * 128, S], BF16, kind="Internal").ap()
    v_dram = nc.dram_tensor("v_dram", [NKV * 128, S], BF16, kind="Internal").ap()

    dbg = {}
    if DEBUG:
        dbg["xnT"] = nc.dram_tensor("dbg_xnT", [HID, S], BF16, kind="ExternalOutput").ap()
        dbg["k"] = nc.dram_tensor("dbg_k", [NKV * 128, S], BF16, kind="ExternalOutput").ap()
        dbg["v"] = nc.dram_tensor("dbg_v", [NKV * 128, S], BF16, kind="ExternalOutput").ap()
        dbg["ctx"] = nc.dram_tensor("dbg_ctx", [NH * 128, 1024], BF16, kind="ExternalOutput").ap()
        dbg["xnTh"] = nc.dram_tensor("dbg_xnTh", [HID, 1024], BF16, kind="ExternalOutput").ap()

    p = Prog()
    fin_tokens = []

    with ExitStack() as top:
        sems = {}
        for s_ in ("q", "g", "p", "v", "a", "sa", "sb", "xa", "xb", "ka", "kb", "va", "vb",
                   "wa", "wb", "pa", "pb", "na", "nb", "xr", "xw", "wk", "ck", "cg"):
            sems[s_] = top.enter_context(nc.semaphore("sem_" + s_))
        block = top.enter_context(nc.Block())

        sbt = lambda nm, shape, dt: top.enter_context(nc.sbuf_tensor(nm, shape, dt))
        bkv_sb = sbt("bkv_sb", [128, 16], F32)
        bq_sb = sbt("bq_sb", [128, 32], F32)
        bd_sb = sbt("bd_sb", [128, 32], F32)
        prot_sb = sbt("prot_sb", [128, 128], BF16)
        idb_sb = sbt("idb_sb", [128, 128], BF16)
        idf_sb = sbt("idf_sb", [128, 128], F32)
        gtab_sb = sbt("gtab_sb", [128, 64], I32)
        fence_sb = sbt("fence_sb", [128, 2], F32)
        p.emit("sync", lambda e: e.dma_start(out=bkv_sb[:], in_=bkv_h), inc="ck", amt=16)
        p.emit("sync", lambda e: e.dma_start(out=bq_sb[:], in_=bq_h), inc="ck", amt=16)
        p.emit("sync", lambda e: e.dma_start(out=bd_sb[:], in_=bd_h), inc="ck", amt=16)
        p.emit("sync", lambda e: e.dma_start(out=idf_sb[:], in_=iden_h), inc="ck", amt=16)
        p.emit("sync", lambda e: e.dma_start(out=gtab_sb[:], in_=gtab_h), inc="ck", amt=16)
        p.emit("gpsimd", lambda e: e.dma_start(out=prot_sb[:], in_=prot_h), inc="cg", amt=16)
        p.emit("gpsimd", lambda e: e.dma_start(out=idb_sb[:], in_=iden_h), inc="cg", amt=16)
        CK = ("ck", 5 * 16)
        CG = ("cg", 2 * 16)
        p0_toks = [CK, CG]

        def ln_tile(stv, x_buf, out_buf, waits_ld, prev_norm_tok, scr=None, war=()):
            scr = stv["scr"] if scr is None else scr
            t1 = p.emit("vector", lambda e: e.reduce_sum(stv["s1"][:], x_buf[:], axis=AXX),
                        waits=waits_ld, inc="v")
            t2 = p.emit("vector", lambda e, scr=scr: e.scalar_tensor_tensor(
                scr[:], x_buf[:], 0.0, x_buf[:], ALU.add, ALU.mult, accum_out=stv["s2"][:]),
                waits=list(war), inc="v")
            tm = p.emit("vector", lambda e: e.tensor_scalar(
                stv["m"][:], stv["s1"][:], 1.0 / HID, None, ALU.mult), waits=[t1], inc="v")
            tq = p.emit("vector", lambda e: e.tensor_tensor(
                stv["msq"][:], stv["m"][:], stv["m"][:], ALU.mult), waits=[tm], inc="v")
            tv0 = p.emit("vector", lambda e: e.scalar_tensor_tensor(
                stv["var"][:], stv["s2"][:], 1.0 / HID, stv["msq"][:], ALU.mult, ALU.subtract),
                waits=[t2, tq], inc="v")
            tv = p.emit("vector", lambda e: e.tensor_scalar(
                stv["var"][:], stv["var"][:], EPS, None, ALU.add), waits=[tv0], inc="v")
            tl = p.emit("scalar", lambda e: e.activation(stv["lnv"][:], stv["var"][:], AF.Ln),
                        waits=[tv], inc="a")
            ta = p.emit("scalar", lambda e: e.activation(
                stv["rstd"][:], stv["lnv"][:], AF.Exp, scale=-0.5), waits=[tl], inc="a")
            tb = p.emit("vector", lambda e: e.tensor_scalar(
                stv["nbias"][:], stv["m"][:], stv["rstd"][:], -1.0, ALU.mult, ALU.mult),
                waits=[ta], inc="v")
            return p.emit("scalar", lambda e: e.activation(
                out_buf[:], x_buf[:], AF.Identity, bias=stv["nbias"][:], scale=stv["rstd"][:]),
                waits=[tb] + list(war), inc="a")

        tok_kdram = tok_vdram = None

        with ExitStack() as ph:
            sb1 = lambda nm, shape, dt: ph.enter_context(nc.sbuf_tensor(nm, shape, dt))
            x_sb = [sb1(f"x_sb{i}", [128, HID], F32) for i in range(2)]
            xnn_sb = [sb1(f"xnn_sb{i}", [128, HID], BF16) for i in range(2)]
            xnT_sb = sb1("xnT_sb", [128, 32 * 256], BF16)
            wkv_sb = sb1("wkv_sb", [128, 32 * 1024], BF16)
            k_sb = sb1("k_sb", [128, NKV * S], BF16)
            vstage = sb1("vstage", [128, 512], BF16)
            cosk_sb = sb1("cosk_sb", [128, S], F32)
            sink_sb = sb1("sink_sb", [128, S], F32)
            rt1 = sb1("rt1", [128, S], BF16)
            rt2 = sb1("rt2", [128, S], BF16)
            stv = {}
            for nm in ("s1", "s2", "m", "msq", "var", "lnv", "rstd", "nbias"):
                stv[nm] = sb1("st_" + nm, [128, 1], F32)
            stv["scr"] = sb1("st_scr", [128, HID], F32)
            ps1 = ph.enter_context(nc.psum_tensor("ps1", [128, 2048], F32))
            pb1 = ph.enter_context(nc.psum_tensor("pb1", [128, 512], BF16))
            pb2 = ph.enter_context(nc.psum_tensor("pb2", [128, 512], BF16))
            pbx = [pb1, pb2]
            pb_free = [None, None]

            def ld_wkv(cols0, waits):
                def fn(e, cols0=cols0):
                    src = wkvT_h.rearrange("(hc p) f -> p hc f", p=128)
                    return e.dma_start(out=wkv_sb[:], in_=src[:, :, cols0:cols0 + 1024])
                return p.emit("gpsimd", fn, waits=waits, inc="wk", amt=16)

            tok_wkvK = ld_wkv(0, [])
            p.emit("sync", lambda e: e.dma_start(out=cosk_sb[:], in_=cosk_h), inc="q", amt=16)
            COSK = p.emit("sync", lambda e: e.dma_start(out=sink_sb[:], in_=sink_h), inc="q", amt=16)

            norm_tok = [None] * 16
            xload = [None] * 16
            tile_evac = [None] * 16
            kv_evac = {}
            mm_last_k = None
            xnn_tp_last = [None, None]

            for sti in range(8):
                for tt in range(2):
                    t = sti * 2 + tt
                    bi = t % 2
                    wl = [norm_tok[t - 2]] if t >= 2 else list(p0_toks)
                    xload[t] = p.emit("sync", lambda e, t=t, bi=bi: e.dma_start(
                        out=x_sb[bi][:], in_=x_hbm[t * 128:(t + 1) * 128, :]), waits=wl,
                        inc="xa" if bi == 0 else "xb", amt=16)
                    norm_tok[t] = ln_tile(stv, x_sb[bi], xnn_sb[bi], [xload[t]], None,
                                          war=[xnn_tp_last[bi]] if xnn_tp_last[bi] else [])
                    evs = []
                    for b8 in range(8):
                        pbb = pbx[b8 % 2]
                        wts = [norm_tok[t], pb_free[b8 % 2]] + ([CG] if t == 0 and b8 == 0 else [])
                        def tfn(e, b8=b8, pbb=pbb, bi=bi):
                            r = None
                            for j in range(4):
                                r = e.transpose(pbb[:, j * 128:(j + 1) * 128],
                                                xnn_sb[bi][:, (b8 * 4 + j) * 128:(b8 * 4 + j + 1) * 128],
                                                idb_sb[:])
                            return r
                        tp = p.emit("tensor", tfn, waits=wts, inc="p")
                        xnn_tp_last[bi] = tp
                        def efn(e, b8=b8, tt=tt, pbb=pbb):
                            dst = xnT_sb[:].rearrange("p (hc f) -> p hc f", hc=32)
                            return e.tensor_copy(dst[:, b8 * 4:(b8 + 1) * 4, tt * 128:(tt + 1) * 128], pbb[:])
                        ev = p.emit("vector", efn, waits=[tp], inc="v")
                        pb_free[b8 % 2] = ev
                        evs.append(ev)
                    tile_evac[t] = evs[-1]
                    def wfn(e, t=t, tt=tt):
                        dst = xn_dram.rearrange("(hc d) t -> d hc t", d=128)
                        src = xnT_sb[:].rearrange("p (hc f) -> p hc f", hc=32)
                        return e.dma_start(out=dst[:, :, t * 128:(t + 1) * 128],
                                           in_=src[:, :, tt * 128:(tt + 1) * 128])
                    p.emit("sync", wfn, waits=[evs[-1], evs[-2]], inc="xw", amt=16)
                for fc in range(8):
                    pk = ps1[:, (fc % 2) * 512:(fc % 2) * 512 + 256]
                    wts = [tile_evac[sti * 2 + 1], tok_wkvK, kv_evac.get(("K", sti, fc - 2)), CK]
                    def mfn(e, fc=fc, pk=pk):
                        r = None
                        for hc in range(32):
                            r = e.matmul(pk, wkv_sb[:, hc * 1024 + fc * 128: hc * 1024 + (fc + 1) * 128],
                                         xnT_sb[:, hc * 256:(hc + 1) * 256], start=(hc == 0), stop=(hc == 31))
                        return r
                    mm_last_k = p.emit("tensor", mfn, waits=wts, inc="p")
                    kv_evac[("K", sti, fc)] = p.emit("vector", lambda e, fc=fc, sti=sti, pk=pk: e.tensor_scalar(
                        k_sb[:, fc * S + sti * 256: fc * S + sti * 256 + 256], pk,
                        bkv_sb[:, fc:fc + 1], None, ALU.add), waits=[mm_last_k], inc="v")

            rope_last = None
            for g in range(NKV):
                for ch in range(4):
                    c0 = g * S + ch * 512
                    pr = ps1[:, 1024 + (ch % 2) * 512: 1024 + (ch % 2) * 512 + 512]
                    tr = p.emit("tensor", lambda e, c0=c0, pr=pr: e.matmul(
                        pr, prot_sb[:], k_sb[:, c0:c0 + 512], start=True, stop=True),
                        waits=[kv_evac[("K", 7, g)], p.now("v"), COSK], inc="p")
                    p.emit("vector", lambda e, c0=c0, ch=ch: e.tensor_tensor(
                        rt1[:, ch * 512:(ch + 1) * 512], k_sb[:, c0:c0 + 512],
                        cosk_sb[:, ch * 512:(ch + 1) * 512], ALU.mult))
                    p.emit("vector", lambda e, ch=ch, pr=pr: e.tensor_tensor(
                        rt2[:, ch * 512:(ch + 1) * 512], pr, sink_sb[:, ch * 512:(ch + 1) * 512], ALU.mult),
                        waits=[tr])
                    rope_last = p.emit("vector", lambda e, c0=c0, ch=ch: e.tensor_tensor(
                        k_sb[:, c0:c0 + 512], rt1[:, ch * 512:(ch + 1) * 512],
                        rt2[:, ch * 512:(ch + 1) * 512], ALU.add), inc="v")
            def kdfn(e):
                return e.dma_start(out=k_dram.rearrange("(g d) t -> d g t", d=128), in_=k_sb[:].rearrange("p (g t) -> p g t", g=8))
            tok_kdram = p.emit("sync", kdfn, waits=[rope_last], inc="xw", amt=16)
            if DEBUG:
                fin_tokens.append(p.emit("gpsimd", lambda e: e.dma_start(
                    out=dbg["k"].rearrange("(g d) t -> d g t", d=128), in_=k_sb[:].rearrange("p (g t) -> p g t", g=8)),
                    waits=[rope_last], inc="g", amt=16))
                fin_tokens.append(p.emit("gpsimd", lambda e: e.dma_start(out=dbg["xnT"], in_=xn_dram),
                                         waits=[tok_kdram], inc="g", amt=16))

            tok_wkvV = ld_wkv(1024, [mm_last_k])
            v_evac = {}
            xn_rd = [None] * 8
            for sti in range(8):
                wts = [tok_kdram, mm_last_k] if sti == 0 else [v_evac[(sti - 1, 7)]]
                def xrd(e, sti=sti):
                    src = xn_dram.rearrange("(hc d) t -> d hc t", d=128)
                    dst = xnT_sb[:].rearrange("p (hc f) -> p hc f", hc=32)
                    return e.dma_start(out=dst, in_=src[:, :, sti * 256:(sti + 1) * 256])
                xn_rd[sti] = p.emit("sync", xrd, waits=wts + [p.now("p")], inc="xr", amt=16)
                for fc in range(8):
                    pk = ps1[:, (fc % 2) * 512:(fc % 2) * 512 + 256]
                    wts = [xn_rd[sti], tok_wkvV, v_evac.get((sti, fc - 2))]
                    def mfn(e, fc=fc, pk=pk):
                        r = None
                        for hc in range(32):
                            r = e.matmul(pk, wkv_sb[:, hc * 1024 + fc * 128: hc * 1024 + (fc + 1) * 128],
                                         xnT_sb[:, hc * 256:(hc + 1) * 256], start=(hc == 0), stop=(hc == 31))
                        return r
                    mmv = p.emit("tensor", mfn, waits=wts, inc="p")
                    ev = p.emit("vector", lambda e, fc=fc, pk=pk: e.tensor_scalar(
                        vstage[:, 0:256], pk, bkv_sb[:, 8 + fc: 9 + fc], None, ALU.add), waits=[mmv], inc="v")
                    pbb = pbx[fc % 2]
                    def vtf(e, pbb=pbb):
                        r = None
                        for j in range(2):
                            r = e.transpose(pbb[:, j * 128:(j + 1) * 128], vstage[:, j * 128:(j + 1) * 128],
                                            idb_sb[:])
                        return r
                    tpv = p.emit("tensor", vtf, waits=[ev, pb_free[fc % 2]], inc="p")
                    v_evac[(sti, fc)] = p.emit("vector", lambda e, fc=fc, sti=sti, pbb=pbb: e.tensor_copy(
                        k_sb[:, fc * S + sti * 256: fc * S + sti * 256 + 256], pbb[:, 0:256]), waits=[tpv], inc="v")
                    pb_free[fc % 2] = v_evac[(sti, fc)]
            def vdfn(e):
                return e.dma_start(out=v_dram.rearrange("(g d) t -> d g t", d=128), in_=k_sb[:].rearrange("p (g t) -> p g t", g=8))
            tok_vdram = p.emit("sync", vdfn, waits=[v_evac[(7, 7)]], inc="xw", amt=16)
            if DEBUG:
                fin_tokens.append(p.emit("gpsimd", lambda e: e.dma_start(
                    out=dbg["v"].rearrange("(g d) t -> d g t", d=128), in_=k_sb[:].rearrange("p (g t) -> p g t", g=8)),
                    waits=[v_evac[(7, 7)]], inc="g", amt=16))
            p.barrier()

        if n_phases < 2:
            tok = p.emit("sync", lambda e: e.dma_start(out=out_hbm[0:128, :], in_=x_sb[0][:]),
                         inc="q", amt=16)
            p.emit("sync", None, waits=[tok] + fin_tokens)
            p.render(block, sems)
            return nc

        with ExitStack() as ph2:
            sb2 = lambda nm, shape, dt: ph2.enter_context(nc.sbuf_tensor(nm, shape, dt))
            xnT_half = sb2("xnT_half", [128, 32 * 1024], BF16)
            ctx_sb = sb2("ctx_sb", [128, 32 * 1024], BF16)
            ps2 = ph2.enter_context(nc.psum_tensor("ps2", [128, 3584], F32))
            pbT = ph2.enter_context(nc.psum_tensor("pbT", [128, 1024], BF16))
            SA0, SB0 = 1024, 2560

            with ExitStack() as pha:
                sba = lambda nm, shape, dt: pha.enter_context(nc.sbuf_tensor(nm, shape, dt))
                x2 = [sba(f"x2_{i}", [128, HID], F32) for i in range(2)]
                xnn2 = [sba(f"xnn2_{i}", [128, HID], F32) for i in range(2)]
                stv2 = {}
                for nm in ("s1", "s2", "m", "msq", "var", "lnv", "rstd", "nbias"):
                    stv2[nm] = sba("sa_" + nm, [128, 1], F32)
                norm2 = [None] * 8
                slot_free = [None, None]
                last_ev = None
                xnn2_tp_last = [None, None]
                for t in range(8):
                    bi = t % 2
                    wl = [norm2[t - 2]] if t >= 2 else []
                    tl = p.emit("sync", lambda e, t=t, bi=bi: e.dma_start(
                        out=x2[bi][:], in_=xm_hbm[t * 128:(t + 1) * 128, :]), waits=wl,
                        inc="xa" if bi == 0 else "xb", amt=16)
                    norm2[t] = ln_tile(stv2, x2[bi], xnn2[bi], [tl], None, scr=xnn2[bi],
                                       war=[xnn2_tp_last[bi]] if xnn2_tp_last[bi] else [])
                    for b8 in range(8):
                        sl = b8 % 2
                        po = ps2[:, sl * 512:(sl + 1) * 512]
                        def tfn(e, b8=b8, po=po, bi=bi):
                            r = None
                            for j in range(4):
                                r = e.transpose(po[:, j * 128:(j + 1) * 128],
                                                xnn2[bi][:, (b8 * 4 + j) * 128:(b8 * 4 + j + 1) * 128],
                                                idf_sb[:])
                            return r
                        tp = p.emit("tensor", tfn, waits=[norm2[t], slot_free[sl]]
                                    + ([CK] if t == 0 and b8 == 0 else []), inc="p")
                        xnn2_tp_last[bi] = tp
                        def efn(e, b8=b8, t=t, po=po):
                            dst = xnT_half[:].rearrange("p (hc f) -> p hc f", hc=32)
                            return e.tensor_copy(dst[:, b8 * 4:(b8 + 1) * 4, t * 128:(t + 1) * 128], po[:])
                        last_ev = p.emit("vector", efn, waits=[tp], inc="v")
                        slot_free[sl] = last_ev
                xnT_ready = [last_ev]
                p.barrier()
            if DEBUG:
                fin_tokens.append(p.emit("gpsimd", lambda e: e.dma_start(
                    out=dbg["xnTh"].rearrange("(hc d) t -> d hc t", d=128), in_=xnT_half[:].rearrange("p (hc t) -> p hc t", hc=32)),
                    inc="g", amt=16))

            with ExitStack() as phb:
                sbb = lambda nm, shape, dt: phb.enter_context(nc.sbuf_tensor(nm, shape, dt))
                q_sb = sbb("q_sb", [128, 4 * 1024], BF16)
                wq_ch = [sbb(f"wq_ch{i}", [128, 32 * 128], BF16) for i in range(2)]
                masks_sb = sbb("masks_sb", [128, 24 * 128], BF16)
                cosq_sb = sbb("cosq_sb", [128, 1024], F32)
                sinq_sb = sbb("sinq_sb", [128, 1024], F32)
                p.emit("sync", lambda e: e.dma_start(out=cosq_sb[:], in_=cosq_h), inc="ck", amt=16)
                p.emit("sync", lambda e: e.dma_start(out=sinq_sb[:], in_=sinq_h), inc="ck", amt=16)
                CKQ = ("ck", 7 * 16)
                p.emit("gpsimd", lambda e: e.dma_start(
                    out=masks_sb[:].rearrange("p (r k) -> p r k", r=24),
                    in_=masks_h.rearrange("(r p) k -> p r k", p=128)), inc="cg", amt=16)
                CGM = ("cg", 3 * 16)
                kg_sb = [sbb(f"kg{i}", [128, 1280], BF16) for i in range(2)]
                vg_sb = [sbb(f"vg{i}", [128, 1280], BF16) for i in range(2)]
                probs_sb = [sbb(f"pr{i}", [128, 1280], BF16) for i in range(2)]
                probsN_sb = [sbb(f"pn{i}", [128, 1280], BF16) for i in range(2)]
                probsT_sb = [sbb(f"pt{i}", [128, 1280], BF16) for i in range(2)]
                rq1 = sbb("rq1", [128, 1024], BF16)
                rq2 = sbb("rq2", [128, 1024], BF16)
                lt = [sbb(f"lt{i}", [128, 1], F32) for i in range(2)]
                nmax = [sbb(f"nmax{i}", [128, 1], F32) for i in range(2)]
                ebias = [sbb(f"eb{i}", [128, 1], F32) for i in range(2)]
                rinv = [sbb(f"ri{i}", [128, 1], F32) for i in range(2)]

                mload = CGM

                ATT_ORDER = []
                for j in range(4):
                    for pr_ in ((0, 4), (1, 5), (2, 6), (3, 7)):
                        ATT_ORDER.append((pr_[0], j, "B"))
                        ATT_ORDER.append((pr_[1], j, "A"))

                kv_ready = [tok_kdram, tok_vdram]
                gath_tok = {}
                kg_free = [None, None]
                vg_free = [None, None]
                probs_free = [None, None]
                probsN_free = [None, None]
                probsT_free = [None, None]
                ctx_slot_free = [None, None]
                score_free = {"A": None, "B": None}
                pbT_free = None
                wq_tok = {}
                last_cev = None

                def emit_gathers(g, idx):
                    buf = idx % 2
                    i = ATT_ORDER[idx][0]
                    n, nr = N_TILES[i], N_RUN[i]
                    off = (n - nr) * 128
                    ksem = "ka" if buf == 0 else "kb"
                    vsem = "va" if buf == 0 else "vb"
                    tk = p.emit("gpsimd", lambda e, g=g, i=i, off=off, nr=nr, buf=buf: e.indirect_dma_start(
                        out=kg_sb[buf][:, off:off + nr * 128], out_offset=None,
                        in_=k_dram,
                        in_offset=IndirectOffsetOnAxis(gtab_sb[:, g * 8 + i:g * 8 + i + 1], 1)),
                        waits=kv_ready + [kg_free[buf], CK], inc=ksem, amt=16)
                    tv_ = p.emit("gpsimd", lambda e, g=g, i=i, off=off, nr=nr, buf=buf: e.indirect_dma_start(
                        out=vg_sb[buf][:, off:off + nr * 128], out_offset=None,
                        in_=v_dram,
                        in_offset=IndirectOffsetOnAxis(gtab_sb[:, g * 8 + i:g * 8 + i + 1], 1)),
                        waits=[vg_free[buf]], inc=vsem, amt=16)
                    if i >= 6:
                        tk = p.emit("gpsimd", lambda e, g=g, buf=buf: e.dma_start(
                            out=kg_sb[buf][:, 0:128], in_=k_dram[g * 128:(g + 1) * 128, 384:512]),
                            inc=ksem, amt=16)
                        tv_ = p.emit("gpsimd", lambda e, g=g, buf=buf: e.dma_start(
                            out=vg_sb[buf][:, 0:128], in_=v_dram[g * 128:(g + 1) * 128, 384:512]),
                            inc=vsem, amt=16)
                    gath_tok[(g, idx)] = (tk, tv_)

                emit_gathers(0, 0)
                emit_gathers(0, 1)

                for g in range(NKV):
                    q_rope = [None] * 4
                    for fc in range(4):
                        wb = fc % 2
                        prev_use = wq_tok.get((g, fc - 2)) or wq_tok.get((g - 1, fc + 2))
                        def wqfn(e, g=g, fc=fc, wb=wb):
                            src = wqT_h.rearrange("(hc p) f -> p hc f", p=128)
                            return e.dma_start(out=wq_ch[wb][:],
                                               in_=src[:, :, g * 512 + fc * 128: g * 512 + (fc + 1) * 128])
                        lw = p.emit("gpsimd", wqfn, waits=[prev_use], inc="wa" if wb == 0 else "wb", amt=16)
                        def mfn(e, wb=wb):
                            r = None
                            for hc in range(32):
                                for ts2 in range(2):
                                    r = e.matmul(ps2[:, ts2 * 512:(ts2 + 1) * 512],
                                                 wq_ch[wb][:, hc * 128:(hc + 1) * 128],
                                                 xnT_half[:, hc * 1024 + ts2 * 512: hc * 1024 + ts2 * 512 + 512],
                                                 start=(hc == 0), stop=(hc == 31))
                            return r
                        mm = p.emit("tensor", mfn, waits=[lw, p.now("v")] + xnT_ready, inc="p")
                        for ts2 in range(2):
                            p.emit("vector", lambda e, fc=fc, ts2=ts2, g=g: e.tensor_scalar(
                                q_sb[:, fc * 1024 + ts2 * 512: fc * 1024 + ts2 * 512 + 512],
                                ps2[:, ts2 * 512:(ts2 + 1) * 512],
                                bq_sb[:, g * 4 + fc: g * 4 + fc + 1], None, ALU.add), waits=[mm], inc="v")
                        wq_tok[(g, fc)] = mm
                        ev_tok = p.now("v")
                        rots = []
                        for ch in range(2):
                            pk = ps2[:, ch * 512:(ch + 1) * 512]
                            rots.append(p.emit("tensor", lambda e, fc=fc, ch=ch, pk=pk: e.matmul(
                                pk, prot_sb[:], q_sb[:, fc * 1024 + ch * 512: fc * 1024 + ch * 512 + 512],
                                start=True, stop=True), waits=[ev_tok], inc="p"))
                        p.emit("vector", lambda e, fc=fc: e.tensor_tensor(
                            rq1[:], q_sb[:, fc * 1024:(fc + 1) * 1024], cosq_sb[:], ALU.mult),
                            waits=[ev_tok, CKQ])
                        for ch in range(2):
                            p.emit("vector", lambda e, ch=ch, pk=ps2[:, ch * 512:(ch + 1) * 512]: e.tensor_tensor(
                                rq2[:, ch * 512:(ch + 1) * 512], pk,
                                sinq_sb[:, ch * 512:(ch + 1) * 512], ALU.mult), waits=[rots[ch]])
                        q_rope[fc] = p.emit("vector", lambda e, fc=fc: e.tensor_tensor(
                            q_sb[:, fc * 1024:(fc + 1) * 1024], rq1[:], rq2[:], ALU.add), inc="v")

                    stageA = {}

                    def emit_stageA(idx):
                        i, j, set_ = ATT_ORDER[idx]
                        buf = idx % 2
                        ncols = N_TILES[i] * 128
                        s0 = SA0 if set_ == "A" else SB0
                        wts = [gath_tok[(g, idx)][0], q_rope[j], score_free[set_], mload]
                        def maskmm(e, i=i, s0=s0):
                            r = None
                            seen = set()
                            for sl in range(3):
                                pos = MASK_POS[i][sl]
                                bk = pos // 512
                                st = bk not in seen
                                seen.add(bk)
                                r = e.matmul(ps2[:, s0 + pos: s0 + pos + 128], idb_sb[:],
                                             masks_sb[:, (i * 3 + sl) * 128:(i * 3 + sl + 1) * 128],
                                             start=st, stop=True)
                            return r
                        p.emit("tensor", maskmm, waits=wts)
                        mask_banks = {MASK_POS[i][sl] // 512 for sl in range(3)}
                        def scoremm(e, i=i, j=j, s0=s0, buf=buf, ncols=ncols, mask_banks=mask_banks):
                            r = None
                            c = 0
                            while c < ncols:
                                w = min(512, ncols - c)
                                st = (c // 512) not in mask_banks
                                r = e.matmul(ps2[:, s0 + c: s0 + c + w],
                                             q_sb[:, j * 1024 + i * 128: j * 1024 + (i + 1) * 128],
                                             kg_sb[buf][:, c:c + w], start=st, stop=True)
                                c += w
                            return r
                        sc = p.emit("tensor", scoremm, inc="p")
                        kg_free[buf] = sc
                        stageA[idx] = sc

                    def emit_stageB(idx):
                        nonlocal last_cev, pbT_free
                        i, j, set_ = ATT_ORDER[idx]
                        buf = idx % 2
                        n = N_TILES[i]
                        ncols = n * 128
                        s0 = SA0 if set_ == "A" else SB0
                        sc = stageA[idx]
                        nm_ = p.emit("vector", lambda e, s0=s0, ncols=ncols, buf=buf: e.reduce_max(
                            nmax[buf][:], ps2[:, s0:s0 + ncols], axis=AXX, negate=True), waits=[sc], inc="v")
                        ex = p.emit("scalar", lambda e, s0=s0, ncols=ncols, buf=buf: e.activation(
                            probs_sb[buf][:, 0:ncols], ps2[:, s0:s0 + ncols], AF.Exp,
                            bias=nmax[buf][:], accum_out=lt[buf][:]),
                            waits=[nm_, probs_free[buf]], inc="a")
                        score_free[set_] = ex
                        p.emit("scalar", lambda e, buf=buf: e.activation(
                            ebias[buf][:], lt[buf][:], AF.Ln))
                        tri = p.emit("scalar", lambda e, buf=buf: e.activation(
                            rinv[buf][:], ebias[buf][:], AF.Exp, scale=-1.0), inc="a")
                        pn = p.emit("vector", lambda e, buf=buf, ncols=ncols: e.tensor_scalar(
                            probsN_sb[buf][:, 0:ncols], probs_sb[buf][:, 0:ncols], rinv[buf][:], None,
                            ALU.mult),
                            waits=[probsN_free[buf], tri], inc="v")
                        probs_free[buf] = pn
                        tev = []
                        tps = []
                        nb = (n + 7) // 8
                        for b8 in range(nb):
                            t0, t1 = b8 * 8, min(n, b8 * 8 + 8)
                            wts_t = [pn] + ([tev[-1]] if b8 else [pbT_free])
                            def tfn(e, t0=t0, t1=t1, buf=buf):
                                r = None
                                for tt in range(t0, t1):
                                    r = e.transpose(pbT[:, (tt - t0) * 128:(tt - t0 + 1) * 128],
                                                    probsN_sb[buf][:, tt * 128:(tt + 1) * 128], idb_sb[:])
                                return r
                            tp = p.emit("tensor", tfn, waits=wts_t, inc="p")
                            tps.append(tp)
                            tev.append(p.emit("vector", lambda e, t0=t0, t1=t1, buf=buf: e.tensor_copy(
                                probsT_sb[buf][:, t0 * 128:t1 * 128], pbT[:, 0:(t1 - t0) * 128]),
                                waits=[tp, probsT_free[buf] if b8 == 0 else None], inc="v"))
                        probsN_free[buf] = tps[-1]
                        pbT_free = tev[-1]
                        cslot = idx % 2
                        cps = ps2[:, cslot * 512: cslot * 512 + 128]
                        def cfn(e, n=n, buf=buf, cps=cps):
                            r = None
                            for tt in range(n):
                                r = e.matmul(cps, vg_sb[buf][:, tt * 128:(tt + 1) * 128],
                                             probsT_sb[buf][:, tt * 128:(tt + 1) * 128],
                                             start=(tt == 0), stop=(tt == n - 1))
                            return r
                        cx = p.emit("tensor", cfn, waits=[tev[-1], gath_tok[(g, idx)][1],
                                                          ctx_slot_free[cslot]], inc="p")
                        probsT_free[buf] = cx
                        vg_free[buf] = cx
                        h = g * 4 + j
                        cev = p.emit("vector", lambda e, h=h, i=i, cps=cps: e.tensor_copy(
                            ctx_sb[:, h * 1024 + i * 128: h * 1024 + (i + 1) * 128], cps),
                            waits=[cx], inc="v")
                        ctx_slot_free[cslot] = cev
                        last_cev = cev

                    emit_stageA(0)
                    for idx in range(len(ATT_ORDER)):
                        if idx + 1 < len(ATT_ORDER):
                            emit_stageA(idx + 1)
                        emit_stageB(idx)
                        i, j, set_ = ATT_ORDER[idx]
                        nxt = g * 32 + idx + 2
                        if nxt < NKV * 32:
                            emit_gathers(nxt // 32, nxt % 32)
                p.barrier()
            if DEBUG:
                fin_tokens.append(p.emit("gpsimd", lambda e: e.dma_start(
                    out=dbg["ctx"].rearrange("(h d) t -> d h t", d=128), in_=ctx_sb[:].rearrange("p (h t) -> p h t", h=32)),
                    inc="g", amt=16))

            if n_phases < 3:
                tok = p.emit("gpsimd", lambda e: e.dma_start(out=out_hbm[0:128, :], in_=xnT_half[:, 0:4096]),
                             inc="g", amt=16)
                p.emit("sync", None, waits=[tok] + fin_tokens)
                p.render(block, sems)
                return nc

            with ExitStack() as ph3:
                sb3 = lambda nm, shape, dt: ph3.enter_context(nc.sbuf_tensor(nm, shape, dt))
                wd_ch = [sb3(f"wd{i}", [128, 32 * 128], BF16) for i in range(2)]
                cwp = [sb3(f"cwp{i}", [128, 32 * 128], BF16) for i in range(2)]
                cwn = [sb3(f"cwn{i}", [128, 32 * 128], BF16) for i in range(2)]
                wcomb = [sb3(f"wc{i}", [128, 32 * 128], BF16) for i in range(2)]
                stepn = sb3("stepn", [128, 16 * 128], BF16)
                g_sb = sb3("g_sb", [128, 512], F32)
                sg_sb = sb3("sg_sb", [128, 1024], F32)
                xch = [sb3(f"xch{i}", [128, 128], F32) for i in range(2)]
                och = [sb3(f"och{i}", [128, 128], F32) for i in range(2)]

                ldd, ldpn, wc_tok, dn_tok, ff_tok = {}, {}, {}, {}, {}
                sg_tok, sil_tok = {}, {}
                xadd = {}
                xbuf_free = {}
                och_free = {}
                po_free = {}
                out_cnt = 0

                def cmb_loads(fc):
                    wb = fc % 2
                    ldd[fc] = p.emit("gpsimd", lambda e, fc=fc, wb=wb: e.dma_start(
                        out=wd_ch[wb][:],
                        in_=wdT_h.rearrange("(hc p) f -> p hc f", p=128)[:, :, fc * 128:(fc + 1) * 128]),
                        waits=[dn_tok.get(fc - 2)], inc="wa" if wb == 0 else "wb", amt=16)
                    ldp = p.emit("gpsimd", lambda e, fc=fc, wb=wb: e.dma_start(
                        out=cwp[wb][:],
                        in_=wpT_h.rearrange("(hc p) f -> p hc f", p=128)[:, :, fc * 128:(fc + 1) * 128]),
                        waits=[wc_tok.get(fc - 2)], inc="pa" if wb == 0 else "pb", amt=16)
                    ldn = p.emit("gpsimd", lambda e, fc=fc, wb=wb: e.dma_start(
                        out=cwn[wb][:],
                        in_=wnT_h.rearrange("(hc p) f -> p hc f", p=128)[:, :, fc * 128:(fc + 1) * 128]),
                        inc="na" if wb == 0 else "nb", amt=16)
                    ldpn[fc] = (ldp, ldn)

                cmb_loads(0)
                cmb_loads(1)
                for fc in range(32):
                    wb = fc % 2
                    for hh in range(2):
                        tsn = p.emit("vector", lambda e, wb=wb, hh=hh: e.tensor_scalar(
                            stepn[:], cwn[wb][:, hh * 2048:(hh + 1) * 2048], 0.0, None, ALU.is_gt),
                            waits=[ldpn[fc][0], ldpn[fc][1], ff_tok.get(fc - 2)] if hh == 0 else [], inc="v")
                        wc_tok[fc] = p.emit("vector", lambda e, wb=wb, hh=hh: e.scalar_tensor_tensor(
                            wcomb[wb][:, hh * 2048:(hh + 1) * 2048], cwp[wb][:, hh * 2048:(hh + 1) * 2048],
                            0.0, stepn[:], ALU.is_gt, ALU.subtract), waits=[tsn], inc="v")
                    for ts2 in range(2):
                        pk = ps2[:, ts2 * 512:(ts2 + 1) * 512]
                        def dfn(e, wb=wb, ts2=ts2, pk=pk):
                            r = None
                            for hc in range(32):
                                r = e.matmul(pk, wd_ch[wb][:, hc * 128:(hc + 1) * 128],
                                             ctx_sb[:, hc * 1024 + ts2 * 512: hc * 1024 + ts2 * 512 + 512],
                                             start=(hc == 0), stop=(hc == 31))
                            return r
                        dn_tok[fc] = p.emit("tensor", dfn,
                                            waits=[ldd[fc], sg_tok.get((fc - 1, ts2))], inc="p")
                        pf = ps2[:, 1024 + ts2 * 512: 1024 + ts2 * 512 + 512]
                        def ffn_(e, wb=wb, ts2=ts2, pf=pf):
                            r = None
                            for hc in range(32):
                                r = e.matmul(pf, wcomb[wb][:, hc * 128:(hc + 1) * 128],
                                             xnT_half[:, hc * 1024 + ts2 * 512: hc * 1024 + ts2 * 512 + 512],
                                             start=(hc == 0), stop=(hc == 31))
                            return r
                        ff_tok[fc] = p.emit("tensor", ffn_,
                                            waits=[wc_tok[fc], sil_tok.get((fc - 1, ts2))], inc="p")
                        sil_tok[(fc, ts2)] = p.emit("scalar", lambda e, pf=pf: e.activation(
                            g_sb[:], pf, AF.Silu),
                            waits=[ff_tok[fc]] + ([sg_tok[(fc, 0)]] if ts2
                                                  else ([sg_tok[(fc - 1, 1)]] if fc else [])), inc="a")
                        p.emit("vector", lambda e, fc=fc, ts2=ts2, pk=pk: e.scalar_tensor_tensor(
                            sg_sb[:, ts2 * 512:(ts2 + 1) * 512], pk, bd_sb[:, fc:fc + 1],
                            g_sb[:], ALU.add, ALU.mult),
                            waits=[dn_tok[fc], sil_tok[(fc, ts2)]], inc="v")
                        sg_tok[(fc, ts2)] = p.emit("vector", lambda e: e.tensor_copy(
                            fence_sb[:, 0:1], fence_sb[:, 1:2]), inc="v")
                    for tt in range(8):
                        slot = tt % 2
                        po = ps2[:, 2048 + slot * 512: 2048 + slot * 512 + 128]
                        xb = tt % 2
                        seq = fc * 8 + tt
                        xld = p.emit("sync", lambda e, fc=fc, tt=tt, xb=xb: e.dma_start(
                            out=xch[xb][:], in_=xm_hbm[tt * 128:(tt + 1) * 128, fc * 128:(fc + 1) * 128]),
                            waits=[xbuf_free.get(xb)], inc="xa" if xb == 0 else "xb", amt=16)
                        p.emit("tensor", lambda e, tt=tt, po=po: e.transpose(
                            po, sg_sb[:, tt * 128:(tt + 1) * 128], idf_sb[:]),
                            waits=[sg_tok[(fc, tt // 4)], po_free.get(slot)], inc="p")
                        tp = p.emit("tensor", lambda e, tt=tt, po2=ps2[:, 2048 + slot * 512 + 128: 2048 + slot * 512 + 256]: e.transpose(
                            po2, sg_sb[:, tt * 128:(tt + 1) * 128], idf_sb[:]), inc="p")
                        ad = p.emit("vector", lambda e, slot=slot, po=po, xb=xb: e.tensor_tensor(
                            och[slot][:], po, xch[xb][:], ALU.add),
                            waits=[tp, xld, och_free.get(slot)], inc="v")
                        po_free[slot] = ad
                        xbuf_free[xb] = ad
                        st_ = p.emit("scalar", lambda e, fc=fc, tt=tt, slot=slot: e.dma_start(
                            out=out_hbm[tt * 128:(tt + 1) * 128, fc * 128:(fc + 1) * 128], in_=och[slot][:]),
                            waits=[ad], inc="sa" if slot == 0 else "sb", amt=16)
                        och_free[slot] = st_
                        out_cnt += 16
                    if fc + 2 < 32:
                        cmb_loads(fc + 2)

                p.emit("scalar", None, waits=[("sa", out_cnt // 2), ("sb", out_cnt // 2)] + fin_tokens)
                p.emit("sync", None, waits=[("sa", out_cnt // 2), ("sb", out_cnt // 2)])
                p.render(block, sems)

    return nc



def _tables():
    inv = 1.0 / (1e6 ** (np.arange(0, HD, 2, dtype=np.float64) / HD))
    t = np.arange(S, dtype=np.float64)
    fr = np.outer(inv, t)
    cosk = np.concatenate([np.cos(fr), np.cos(fr)], 0).astype(np.float32)
    sink = np.concatenate([np.sin(fr), np.sin(fr)], 0).astype(np.float32)
    return cosk, sink


def _masks(parity):
    causal = np.where(np.tril(np.ones((128, 128), bool)), 0.0, NEG).astype(np.float32)
    full = np.full((128, 128), NEG, np.float32)
    zero = np.zeros((128, 128), np.float32)
    m = np.zeros((8, 3, 128, 128), np.float32)
    for i in range(8):
        pg = 2 * i + parity
        if i < 4:
            if parity == 0:
                m[i, 0], m[i, 1], m[i, 2] = causal, full, zero
            else:
                m[i, 0], m[i, 1], m[i, 2] = zero, causal, zero
        else:
            ext = np.zeros((128, 128), np.float32)
            ext[:, 0:64] = NEG
            if (2 * pg - 14) % 8 != 0:
                ext[64:128, 64:128] = NEG
            if i < 6:
                m[i, 0], m[i, 1], m[i, 2] = ext, causal, zero
            else:
                vert = np.zeros((128, 128), np.float32)
                vert[:, 0:64] = NEG
                m[i, 0], m[i, 1], m[i, 2] = vert, ext, causal
    return np.ascontiguousarray(m.reshape(8 * 3 * 128, 128))


def make_inputs(L0_attn_hidden_states, L0_attn_w_qkv, L0_attn_b_qkv, L0_attn_w_dense,
                L0_attn_b_dense, L0_ffn_w_pos, L0_ffn_w_neg):
    x_full = np.ascontiguousarray(np.asarray(L0_attn_hidden_states, dtype=np.float32))
    w_qkv = np.asarray(L0_attn_w_qkv, dtype=np.float32)
    b_qkv = np.asarray(L0_attn_b_qkv, dtype=np.float32)
    qrows = np.concatenate([np.arange(g * 768, g * 768 + 512) for g in range(NKV)])
    krows = np.concatenate([np.arange(g * 768 + 512, g * 768 + 640) for g in range(NKV)])
    vrows = np.concatenate([np.arange(g * 768 + 640, g * 768 + 768) for g in range(NKV)])
    wqT = np.ascontiguousarray(w_qkv[qrows].T)
    wkvT = np.ascontiguousarray(np.concatenate([w_qkv[krows], w_qkv[vrows]], 0).T)
    wdT = np.ascontiguousarray(np.asarray(L0_attn_w_dense, np.float32).T)
    wpT = np.ascontiguousarray(np.asarray(L0_ffn_w_pos, np.float32).T)
    wnT = np.ascontiguousarray(np.asarray(L0_ffn_w_neg, np.float32).T)
    bkv_re = np.ascontiguousarray(np.concatenate([b_qkv[krows], b_qkv[vrows]]).reshape(16, 128).T)
    bq_re = np.ascontiguousarray(b_qkv[qrows].reshape(32, 128).T)
    bd_re = np.ascontiguousarray(np.asarray(L0_attn_b_dense, np.float32).reshape(32, 128).T)
    cosk, sink = _tables()
    prot = np.zeros((128, 128), np.float32)
    for mm in range(64):
        prot[mm + 64, mm] = -1.0
        prot[mm, mm + 64] = 1.0
    iden = np.eye(128, dtype=np.float32)

    in_maps = []
    for c in range(8):
        b, parity = c // 2, c % 2
        rows = np.concatenate([np.arange(128 * (2 * i + parity), 128 * (2 * i + parity) + 128)
                               for i in range(8)])
        gtab = np.empty((128, 64), np.int32)
        for g in range(8):
            for i in range(8):
                gtab[:, g * 8 + i] = (g * 128 + np.arange(128, dtype=np.int32)) * S                     + RUN_START[parity][i] * 128
        in_maps.append(dict(
            x=x_full[b], x_mine=np.ascontiguousarray(x_full[b][rows]),
            wkvT=wkvT, wqT=wqT, wdT=wdT, wpT=wpT, wnT=wnT,
            bkv=bkv_re, bq=bq_re, bd=bd_re, cosk=cosk, sink=sink,
            cosq=np.ascontiguousarray(cosk[:, rows] * np.float32(SM)),
            sinq=np.ascontiguousarray(sink[:, rows] * np.float32(SM)),
            prot=prot, iden=iden, masks=_masks(parity), gtab=gtab))
    return in_maps


def kernel(L0_attn_hidden_states, L0_attn_w_qkv, L0_attn_b_qkv, L0_attn_w_dense,
           L0_attn_b_dense, L0_ffn_w_pos, L0_ffn_w_neg):
    in_maps = make_inputs(L0_attn_hidden_states, L0_attn_w_qkv, L0_attn_b_qkv,
                          L0_attn_w_dense, L0_attn_b_dense, L0_ffn_w_pos, L0_ffn_w_neg)
    nc = build_program()
    res = run_bass_kernel_spmd(nc, in_maps, core_ids=list(range(8)))
    out = np.empty((B, S, HID), np.float32)
    for c in range(8):
        b, parity = c // 2, c % 2
        r = res.results[c]["out"]
        for i in range(8):
            out[b, 128 * (2 * i + parity):128 * (2 * i + parity) + 128, :] = \
                r[i * 128:(i + 1) * 128, :]
    return out
